# version 18
# baseline (speedup 1.0000x reference)
import numpy as np
import ml_dtypes
import concourse.bass as bass
import concourse.mybir as mybir
from concourse.bass_utils import run_bass_kernel_spmd

F32 = mybir.dt.float32
BF16 = mybir.dt.bfloat16
AF = mybir.ActivationFunctionType
ALU = mybir.AluOpType

P = 128
D = 1024
NCH = 8
HEADS = 16
HD = 64
WIN = 16
TT = 256
T_HALO = 128
N_TILES = 9
T_OWN = 2048
EPS = 1e-5
SCALE = 0.125

ENG2BLOCK = {"pe": "tensor", "act": "scalar", "dve": "vector", "pool": "gpsimd", "sync": "sync"}
TAPS = False


class Prog:
    SERIAL = False
    DMA_LANES = 4

    def __init__(self):
        self.ops = []
        self.tag = ""

    def add(self, eng, fn, reads=(), writes=()):
        self.ops.append(dict(eng=eng, fn=fn, reads=tuple(reads), writes=tuple(writes),
                             gi=len(self.ops), tag=self.tag))

    def capture(self, fn):
        saved = self.ops
        self.ops = []
        fn()
        out = self.ops
        self.ops = saved
        return out

    def interleave(self, fns):
        lists = [self.capture(f) for f in fns]
        import itertools
        for grp in itertools.zip_longest(*lists):
            for op in grp:
                if op is not None:
                    op["gi"] = len(self.ops)
                    self.ops.append(op)

    def dma(self, eng, fn, reads=(), writes=()):
        assert eng in ("sync", "pool")
        self.ops.append(dict(eng=eng, fn=fn, reads=tuple(reads), writes=tuple(writes),
                             gi=len(self.ops), dma=True, tag=self.tag))

    def resolve(self):
        if Prog.SERIAL:
            kept = [set() for _ in self.ops]
            for op in self.ops:
                if op["gi"] > 0:
                    kept[op["gi"]].add(op["gi"] - 1)
            return kept
        last_writer = {}
        readers = {}
        deps = [set() for _ in self.ops]
        for op in self.ops:
            gi = op["gi"]
            d = deps[gi]
            for r in op["reads"]:
                if r in last_writer:
                    d.add(last_writer[r])
            for w in op["writes"]:
                if w in last_writer:
                    d.add(last_writer[w])
                for rd in readers.get(w, ()):
                    d.add(rd)
            for r in op["reads"]:
                readers.setdefault(r, []).append(gi)
            for w in op["writes"]:
                last_writer[w] = gi
                readers[w] = [gi] if w in op["reads"] else []
        kept = [set() for _ in self.ops]
        for op in self.ops:
            gi = op["gi"]
            for dj in deps[gi]:
                if dj == gi:
                    continue
                pj = self.ops[dj]
                if pj.get("dma") or pj["eng"] != op["eng"] or pj.get("memset"):
                    assert dj < gi
                    kept[gi].add(dj)
        return kept

    def emit_plan(self):
        kept = self.resolve()
        needed = set()
        for ds in kept:
            needed.update(ds)
        cum = {}
        counters = {e: 0 for e in ("pe", "act", "dve", "pool_c")}
        S = Prog.DMA_LANES
        lane_cum = {(q, l): 0 for q in ("sync", "pool") for l in range(S)}
        dma_idx = {"sync": 0, "pool": 0}
        qprefix = {"sync": [], "pool": []}
        qpos = {}
        for op in self.ops:
            gi, eng = op["gi"], op["eng"]
            if op.get("dma"):
                lane = dma_idx[eng] % S
                dma_idx[eng] += 1
                lane_cum[(eng, lane)] += 16
                sem = f"dma_{eng}{lane}"
                cum[gi] = (sem, lane_cum[(eng, lane)])
                op["inc"] = (sem, 16)
                qprefix[eng].append(tuple(lane_cum[(eng, l)] for l in range(S)))
                qpos[gi] = len(qprefix[eng]) - 1
            elif gi in needed:
                key = "pool_c" if eng == "pool" else eng
                counters[key] += 1
                cum[gi] = (key, counters[key])
                op["inc"] = (key, 1)
            else:
                op["inc"] = None
        watermark = {}
        for op in self.ops:
            gi, eng = op["gi"], op["eng"]
            ws = {}
            if op.get("dma"):
                sem_self, v_self = cum[gi]
                if v_self > 16:
                    ws[sem_self] = v_self - 16
            for dj in kept[gi]:
                sem, val = cum[dj]
                if sem.startswith("dma_"):
                    q = self.ops[dj]["eng"]
                    for l, lv in enumerate(qprefix[q][qpos[dj]]):
                        s2 = f"dma_{q}{l}"
                        ws[s2] = max(ws.get(s2, 0), lv)
                else:
                    ws[sem] = max(ws.get(sem, 0), val)
            final = []
            for sem, val in ws.items():
                k = (eng, sem)
                if watermark.get(k, 0) < val:
                    watermark[k] = val
                    final.append((sem, val))
            op["waits"] = final
        per_eng = {b: [] for b in ENG2BLOCK.values()}
        for op in self.ops:
            per_eng[ENG2BLOCK[op["eng"]]].append(op)
        self.dma_totals = {f"dma_{q}{l}": lane_cum[(q, l)]
                           for q in ("sync", "pool") for l in range(S)}
        return per_eng


def tile_tok(t):
    return T_HALO if t == 0 else TT


def toff(t):
    return 0 if t == 0 else T_HALO + (t - 1) * TT


def build_nc():
    nc = bass.Bass("TRN2", target_bir_lowering=False)
    TLOC = T_HALO + T_OWN

    x_in = nc.dram_tensor("x_sl", [D, TLOC], F32, kind="ExternalInput").ap()
    wts_in = {nm: nc.dram_tensor(nm + "T", [D, D], F32, kind="ExternalInput").ap()
              for nm in ("wr", "wk", "wv", "wq", "wkk", "wkv", "wo")}
    vec_in = {nm: nc.dram_tensor(nm, [P, NCH], F32, kind="ExternalInput").ap()
              for nm in ("murs", "muks", "gammas", "betas")}
    mcur_in = nc.dram_tensor("mask_cur", [P, P], F32, kind="ExternalInput").ap()
    mprev_in = nc.dram_tensor("mask_prev", [16, 256], F32, kind="ExternalInput").ap()
    bmask_in = nc.dram_tensor("bnd_mask", [P, 1], F32, kind="ExternalInput").ap()
    out_hbm = nc.dram_tensor("out_fm", [D, T_OWN], F32, kind="ExternalOutput").ap()
    taps = {}
    if TAPS:
        for nm in ("tap1", "tap2"):
            taps[nm] = nc.dram_tensor(nm, [D, TLOC], F32, kind="ExternalOutput").ap() \
                .rearrange("(c p) t -> p c t", p=P)

    x_v = x_in.rearrange("(c p) t -> p c t", p=P)
    out_v = out_hbm.rearrange("(c p) t -> p c t", p=P)

    prog = Prog()
    ctx = []

    def sb(name, shape, dtype):
        g = nc.sbuf_tensor(name, shape, dtype)
        t = g.__enter__()
        ctx.append(g)
        return t

    def mkpsum(name):
        g = nc.psum_tensor(name, [P, 512], F32)
        t = g.__enter__()
        ctx.append(g)
        return t

    W = {nm: sb("w_" + nm, [P, NCH * 1024], BF16) for nm in
         ("wr", "wk", "wv", "wq", "wkk", "wkv", "wo")}
    stream = [sb(f"stream{i}", [P, NCH * TT], F32) for i in range(3)]
    ln1 = sb("ln1", [P, NCH * 257], BF16)
    ln3 = sb("ln3", [P, NCH * 257], BF16)
    xr1 = sb("xr1", [P, NCH * TT], BF16)
    xk1 = sb("xk1", [P, NCH * TT], BF16)
    xr3 = sb("xr3", [P, NCH * TT], BF16)
    xk3 = sb("xk3", [P, NCH * TT], BF16)
    lna = sb("lna", [P, NCH * 272], BF16)
    qbf = sb("qbf", [P, NCH * TT], BF16)
    krng = sb("krng", [P, NCH * 272], BF16)
    vtm = [sb(f"vtm{i}", [P, 1024], BF16) for i in range(2)]
    vbnd = sb("vbnd", [16, 1024], BF16)
    tauM = [sb(f"tauM{i}", [P, 512], BF16) for i in range(2)]
    tauP = sb("tauP", [16, 256], BF16)
    msgf = qbf2 = None
    mu_exp = {"murs": sb("mu_r_bf", [P, NCH], BF16),
              "muks": sb("mu_k_bf", [P, NCH], BF16)}
    pool = [sb(f"pb{i}", [P, NCH * TT], BF16) for i in range(3)]
    cpx = sb("cpx", [P, NCH * TT], BF16)
    cps = sb("cps", [P, 4 * TT], BF16)
    dbuf = sb("dbuf", [P, NCH * TT], BF16)
    rows = sb("rows", [1, 768], F32)
    mcur = sb("mcur", [P, P], BF16)
    mprev = sb("mprev", [16, 256], BF16)
    cols_sb = {nm: sb(nm + "_s", [P, NCH], F32) for nm in ("murs", "muks", "gammas", "betas")}
    bmask = sb("bmask", [P, 1], F32)
    ones1 = sb("ones1", [P, 1], BF16)
    onesk1f = sb("onesk1f", [1, P], F32)
    epsc = sb("epsc", [1, 1], F32)

    PS = {nm: mkpsum("ps_" + nm) for nm in ("a", "b", "s1", "s2", "m1", "m2", "bc", "st")}

    sems = {}
    for n in ("pe", "act", "dve", "pool_c") + tuple(
            f"dma_{q}{l}" for q in ("sync", "pool") for l in range(Prog.DMA_LANES)):
        g = nc.semaphore("sem_" + n)
        sems[n] = g.__enter__()
        ctx.append(g)

    def proj_bank(i):
        return (("a", "b", "s1", "s2")[i % 4], 256 * ((i // 4) % 2))

    def ch3(tensor, width, col0, n, c=NCH):
        v = tensor.ap().rearrange("p (c w) -> p c w", c=c)
        return v[:, :, col0:col0 + n]

    add = prog.add
    dma = prog.dma

    for nm in ("murs", "muks", "gammas", "betas"):
        dma("sync", (lambda d_, s_: lambda B: B.sync.dma_start(out=d_[:], in_=s_))(cols_sb[nm], vec_in[nm]),
            writes=[nm])
    dma("sync", lambda B: B.sync.dma_start(out=bmask[:], in_=bmask_in), writes=["bmask"])

    def add_memset(fn, writes):
        prog.add("dve", fn, (), writes)
        prog.ops[-1]["memset"] = True
    add_memset(lambda B: B.dve.memset(ones1[:], 1.0), ["ones1"])
    add_memset(lambda B: B.dve.memset(onesk1f[:], 1.0), ["onesk1"])
    add_memset(lambda B: B.dve.memset(epsc[:], EPS), ["epsc"])
    add_memset(lambda B: B.dve.memset(ch3(ln1, 257, 0, 1), 0.0), ["ln1"])
    add_memset(lambda B: B.dve.memset(ch3(ln3, 257, 0, 1), 0.0), ["ln3"])
    add_memset(lambda B: B.dve.memset(ch3(lna, 272, 0, 16), 0.0), ["lna"])
    add_memset(lambda B: B.dve.memset(ch3(krng, 272, 0, 16), 0.0), ["krng"])
    add_memset(lambda B: B.dve.memset(vbnd[:], 0.0), ["vbnd"])
    add("act", lambda B: B.act.activation(out=rows[0:1, 0:2], in_=epsc[0:1, 0:1].broadcast_to([1, 2]),
                                          func=AF.Sigmoid), reads=["epsc"], writes=["rows"])
    add("act", lambda B: B.act.activation(out=rows[0:1, 0:2], in_=epsc[0:1, 0:1].broadcast_to([1, 2]),
                                          func=AF.Sqrt), reads=["epsc"], writes=["rows"])
    for nm2 in ("a", "b", "s1", "s2", "m1", "m2", "bc", "st"):
        add("dve", (lambda n2: lambda B: B.dve.memset(PS[n2][:], 0.0))(nm2),
            writes=[f"ps_{nm2}"])
        prog.ops[-1]["memset"] = True
    for munm in ("murs", "muks"):
        dma("pool", (lambda d_, s_: lambda B: B.pool.dma_start(out=d_[:], in_=s_))(
            mu_exp[munm], vec_in[munm]), writes=["mu_" + munm])

    def _tagged(fn):
        def w(t, *a, **k):
            old = prog.tag
            prog.tag = f"{fn.__name__[3:]}({t})"
            fn(t, *a, **k)
            prog.tag = old
        w.__name__ = fn.__name__
        return w

    @_tagged
    def st_xdma(t):
        TTt = tile_tok(t)
        strm = stream[t % 3]
        dst = ch3(strm, TT, 0, TTt)
        src = x_v[:, :, toff(t):toff(t) + TTt]
        xgate = [f"xok{t - 1}"] if t > 0 else []
        dma("sync", (lambda d_, s_: lambda B: B.sync.dma_start(out=d_, in_=s_))(dst, src),
            reads=xgate, writes=[f"stream{t % 3}"])

    def ln_stats(t, src3, src_rgs, src_flat=None, first_writes=(), stbank="st"):
        TTt = tile_tok(t)
        if src_flat is None:
            for h in range(2):
                s3 = src3[:, 4 * h:4 * h + 4, :]
                xo = ch3(cpx, TT, 0, TTt)[:, 4 * h:4 * h + 4, :]
                add("act", (lambda o_, i_: lambda B: B.act.activation(
                    out=o_, in_=i_, func=AF.Copy))(xo, s3),
                    reads=src_rgs, writes=[f"pcx{h}"] + (list(first_writes) if h == 0 else []))
            sum_src, sum_rg = cpx, lambda j: [f"pcx{j // 4}"]
        else:
            sum_src, sum_rg = src_flat, lambda j: list(src_rgs)
        sq3 = ch3(cps, TT, 0, TTt, c=4)
        add("act", (lambda o_, i_: lambda B: B.act.activation(
            out=o_, in_=i_, func=AF.Square))(sq3, src3[:, 0:4, :]),
            reads=src_rgs, writes=["cps0"])
        for j in range(NCH):
            add("pe", (lambda j_, s_, e_: lambda B: B.pe.matmul(
                PS[stbank][0:1, 0:TTt], ones1[:], sum_src[:, j_ * TT:j_ * TT + TTt],
                start=s_, stop=e_))(j, j == 0, j == NCH - 1),
                reads=sum_rg(j) + ["ones1"], writes=[f"ps_{stbank}"])
        for j in range(4):
            add("pe", (lambda j_, s_: lambda B: B.pe.matmul(
                PS[stbank][0:1, 256:256 + TTt], ones1[:], cps[:, j_ * TT:j_ * TT + TTt],
                start=s_, stop=False))(j, j == 0),
                reads=["cps0", "ones1"], writes=[f"ps_{stbank}"])
        add("act", (lambda o_, i_: lambda B: B.act.activation(
            out=o_, in_=i_, func=AF.Square))(sq3, src3[:, 4:8, :]),
            reads=src_rgs, writes=["cps0", "cps1"])
        for j in range(4):
            add("pe", (lambda j_, e_: lambda B: B.pe.matmul(
                PS[stbank][0:1, 256:256 + TTt], ones1[:], cps[:, j_ * TT:j_ * TT + TTt],
                start=False, stop=e_))(j, j == 3),
                reads=["cps1", "ones1"], writes=[f"ps_{stbank}"])

    def ln_rows_bcast(t, stbank="st", bcbank="bc"):
        TTt = tile_tok(t)
        stap = PS[stbank].ap()
        add("act", (lambda s_, n_: lambda B: B.act.activation(
            out=rows[0:1, 512:512 + n_], in_=s_[0:1, 0:n_], func=AF.Square,
            scale=0.03125))(stap, TTt),
            reads=[f"ps_{stbank}"], writes=["rows"])
        add("dve", (lambda s_, n_: lambda B: B.dve.tensor_tensor(
            out=rows[0:1, 0:n_], in0=s_[0:1, 256:256 + n_],
            in1=rows[0:1, 512:512 + n_], op=ALU.subtract))(stap, TTt),
            reads=[f"ps_{stbank}", "rows"], writes=["rows"])
        add("act", (lambda n_: lambda B: B.act.activation(
            out=rows[0:1, 512:512 + n_], in_=rows[0:1, 0:n_], func=AF.Sqrt,
            scale=1.0 / D, bias=epsc[0:1, 0:1]))(TTt), reads=["rows", "epsc"], writes=["rows"])
        add("dve", (lambda n_: lambda B: B.dve.reciprocal(
            out=rows[0:1, 256:256 + n_], in_=rows[0:1, 512:512 + n_]))(TTt),
            reads=["rows"], writes=["rows"])
        add("act", (lambda s_, n_: lambda B: B.act.activation(
            out=rows[0:1, 0:n_], in_=s_[0:1, 0:n_], func=AF.Copy,
            scale=1.0 / D))(stap, TTt),
            reads=[f"ps_{stbank}", "rows"], writes=["rows"])
        add("dve", (lambda n_: lambda B: B.dve.tensor_tensor(
            out=rows[0:1, 512:512 + n_], in0=rows[0:1, 0:n_],
            in1=rows[0:1, 256:256 + n_], op=ALU.mult))(TTt),
            reads=["rows"], writes=["rows"])
        if TTt == TT:
            add("pe", lambda B: B.pe.matmul(
                PS[bcbank][:, 0:512], onesk1f[:], rows[0:1, 256:768],
                start=True, stop=True),
                reads=["rows", "onesk1"], writes=[f"ps_{bcbank}"])
        else:
            for (dst0, s_) in ((0, 256), (256, 512)):
                add("pe", (lambda d_, s2_, n_: lambda B: B.pe.matmul(
                    PS[bcbank][:, d_:d_ + n_], onesk1f[:], rows[0:1, s2_:s2_ + n_],
                    start=True, stop=True))(dst0, s_, TTt),
                    reads=["rows", "onesk1"], writes=[f"ps_{bcbank}"])

    def ln_apply(t, src3, src_rgs, dst3, dst_rg, bcbank="bc"):
        TTt = tile_tok(t)
        rb4 = PS[bcbank].ap()[:, 0:TTt].unsqueeze(1).broadcast_to([P, 4, TTt])
        mb4 = PS[bcbank].ap()[:, 256:256 + TTt].unsqueeze(1).broadcast_to([P, 4, TTt])
        for h in range(2):
            s3 = src3[:, 4 * h:4 * h + 4, :]
            d3 = dst3[:, 4 * h:4 * h + 4, :]
            add("dve", (lambda s_, d_: lambda B: B.dve.tensor_tensor(
                out=d_, in0=s_, in1=rb4, op=ALU.mult))(s3, d3),
                reads=list(src_rgs) + [f"ps_{bcbank}"], writes=[dst_rg])
            add("dve", (lambda d_: lambda B: B.dve.tensor_tensor(
                out=d_, in0=d_, in1=mb4, op=ALU.subtract))(d3),
                reads=[dst_rg, f"ps_{bcbank}"], writes=[dst_rg])

    def ring_copy(t, tensor, rg, width, halo, masked=False):
        prev_n = tile_tok(t - 1)
        src = ch3(tensor, width, prev_n, halo)
        dstc = ch3(tensor, width, 0, halo)
        if masked:
            add("dve", (lambda s_, d_: lambda B: B.dve.tensor_scalar(
                out=d_, in0=s_, scalar1=bmask[:, 0:1], scalar2=None, op0=ALU.mult))(src, dstc),
                reads=[rg, "bmask"], writes=[rg])
        else:
            add("dve", (lambda s_, d_: lambda B: B.dve.tensor_copy(out=d_, in_=s_))(src, dstc),
                reads=[rg], writes=[rg])

    def ln_mixes(t, ln_t, ln_rg, xr_t, xr_rg, xk_t, xk_rg):
        TTt = tile_tok(t)
        lnv = ch3(ln_t, 257, 1, TTt)
        lpv = ch3(ln_t, 257, 0, TTt)
        dv = ch3(dbuf, TT, 0, TTt)
        add("dve", (lambda a_, b_, d_: lambda B: B.dve.tensor_tensor(
            out=d_, in0=a_, in1=b_, op=ALU.subtract))(lnv, lpv, dv),
            reads=[ln_rg], writes=["dbuf"])
        for (xt, xrg, munm) in ((xr_t, xr_rg, "murs"), (xk_t, xk_rg, "muks")):
            for j in range(NCH):
                add("dve", (lambda j_, xt_, mu_: lambda B: B.dve.scalar_tensor_tensor(
                    out=xt_[:, j_ * TT:j_ * TT + TTt],
                    in0=dbuf[:, j_ * TT:j_ * TT + TTt],
                    scalar=mu_[:, j_:j_ + 1],
                    in1=ln_t[:, j_ * 257:j_ * 257 + TTt],
                    op0=ALU.mult, op1=ALU.add))(j, xt, mu_exp[munm].ap()),
                    reads=["dbuf", "mu_" + munm, ln_rg], writes=[xrg])

    def ffn_proj2(t, xr_t, xr_rg, xk_t, xk_rg):
        TTt = tile_tok(t)

        def proj(wname, x_t, x_rg, dstbuf, dst_rg, func, scale):
            for i in range(NCH):
                bank, slot = proj_bank(i)
                for j in range(NCH):
                    add("pe", (lambda b_, s_, i_, j_, w_, x2, n_: lambda B: B.pe.matmul(
                        PS[b_][:, s_:s_ + n_],
                        W[w_][:, j_ * 1024 + i_ * 128: j_ * 1024 + (i_ + 1) * 128],
                        x2[:, j_ * TT:j_ * TT + n_],
                        start=j_ == 0, stop=j_ == NCH - 1))(bank, slot, i, j, wname, x_t, TTt),
                        reads=[f"w_{wname}_{j}", x_rg], writes=[f"ps_{bank}"])
                add("act", (lambda i_, b_, s_, d_, f_, n_, sc_: lambda B: B.act.activation(
                    out=d_[:, i_ * TT:i_ * TT + n_], in_=PS[b_][:, s_:s_ + n_],
                    func=f_, scale=sc_))(i, bank, slot, dstbuf, func, TTt, scale),
                    reads=[f"ps_{bank}"], writes=[dst_rg])
        proj("wr", xr_t, xr_rg, pool[0], "pb0", AF.Sigmoid, 1.0)
        proj("wk", xk_t, xk_rg, pool[1], "pb1", AF.Relu, 0.7071067811865476)

    def ffn_kact(t):
        TTt = tile_tok(t)
        add("act", (lambda n_: lambda B: B.act.activation(
            out=ch3(pool[2], TT, 0, n_), in_=ch3(pool[1], TT, 0, n_), func=AF.Square))(TTt),
            reads=["pb1"], writes=["pb2"])

    def ffn_v(t, strm, st_rg):
        TTt = tile_tok(t)
        for i in range(NCH):
            bank, slot = proj_bank(i)
            for j in range(NCH):
                add("pe", (lambda b_, s_, i_, j_, n_: lambda B: B.pe.matmul(
                    PS[b_][:, s_:s_ + n_],
                    W["wv"][:, j_ * 1024 + i_ * 128: j_ * 1024 + (i_ + 1) * 128],
                    pool[2][:, j_ * TT:j_ * TT + n_],
                    start=j_ == 0, stop=j_ == NCH - 1))(bank, slot, i, j, TTt),
                    reads=["w_wv_" + str(j), "pb2"], writes=[f"ps_{bank}"])
            add("dve", (lambda i_, b_, s_, n_: lambda B: B.dve.tensor_tensor(
                out=pool[1][:, i_ * TT:i_ * TT + n_],
                in0=pool[0][:, i_ * TT:i_ * TT + n_], in1=PS[b_][:, s_:s_ + n_],
                op=ALU.mult))(i, bank, slot, TTt),
                reads=["pb0", f"ps_{bank}"], writes=["pb1"])
            add("dve", (lambda i_, st_, n_: lambda B: B.dve.tensor_tensor(
                out=st_[:, i_ * TT:i_ * TT + n_], in0=pool[1][:, i_ * TT:i_ * TT + n_],
                in1=st_[:, i_ * TT:i_ * TT + n_], op=ALU.add))(i, strm, TTt),
                reads=["pb1", st_rg], writes=[st_rg])

    @_tagged
    def st_P1_stats(t):
        TTt = tile_tok(t)
        strm = stream[t % 3]
        st_rg = f"stream{t % 3}"
        sv = ch3(strm, TT, 0, TTt)
        if t > 0:
            ring_copy(t, ln1, "ln1", 257, 1)
        ln_stats(t, sv, [st_rg], first_writes=[f"xok{t}"])

    @_tagged
    def st_P1_rest(t):
        TTt = tile_tok(t)
        strm = stream[t % 3]
        st_rg = f"stream{t % 3}"
        sv = ch3(strm, TT, 0, TTt)
        ln_rows_bcast(t)
        ln_apply(t, sv, [st_rg], ch3(ln1, 257, 1, TTt), "ln1")
        ln_mixes(t, ln1, "ln1", xr1, "xr1", xk1, "xk1")

    def st_P1(t):
        st_P1_stats(t)
        st_P1_rest(t)

    @_tagged
    def st_M1a(t):
        ffn_proj2(t, xr1, "xr1", xk1, "xk1")
        ffn_kact(t)

    @_tagged
    def st_M1b(t):
        ffn_v(t, stream[t % 3], f"stream{t % 3}")

    @_tagged
    def st_P2(t):
        TTt = tile_tok(t)
        strm = stream[t % 3]
        st_rg = f"stream{t % 3}"
        sv = ch3(strm, TT, 0, TTt)
        if t > 0:
            ring_copy(t, lna, "lna", 272, 16)
            ring_copy(t, krng, "krng", 272, 16)
        if TAPS:
            dtap = taps["tap1"][:, :, toff(t):toff(t) + TTt]
            dma("sync", (lambda d_, s_: lambda B: B.sync.dma_start(out=d_, in_=s_))(dtap, sv),
                reads=[st_rg])
        ln_stats(t, sv, [st_rg])
        ln_rows_bcast(t, bcbank="s1")
        ln_apply(t, sv, [st_rg], ch3(lna, 272, 16, TTt), "lna", bcbank="s1")

    @_tagged
    def st_M2(t):
        TTt = tile_tok(t)

        def aproj(wname, dst_t, dst_rg, width, col0):
            for i in range(NCH):
                bank, slot = proj_bank(i)
                for j in range(NCH):
                    add("pe", (lambda b_, s_, i_, j_, w_, n_: lambda B: B.pe.matmul(
                        PS[b_][:, s_:s_ + n_],
                        W[w_][:, j_ * 1024 + i_ * 128: j_ * 1024 + (i_ + 1) * 128],
                        lna[:, j_ * 272 + 16: j_ * 272 + 16 + n_],
                        start=j_ == 0, stop=j_ == NCH - 1))(bank, slot, i, j, wname, TTt),
                        reads=[f"w_{wname}_{j}", "lna"], writes=[f"ps_{bank}"])
                add("act", (lambda i_, b_, s_, d_, wd_, c0, n_: lambda B: B.act.activation(
                    out=d_[:, i_ * wd_ + c0: i_ * wd_ + c0 + n_],
                    in_=PS[b_][:, s_:s_ + n_], func=AF.Copy))(i, bank, slot, dst_t, width, col0, TTt),
                    reads=[f"ps_{bank}"], writes=[dst_rg])
        aproj("wq", qbf, "qbf", TT, 0)
        aproj("wkk", krng, "krng", 272, 16)

    @_tagged
    def st_A2(t):
        TTt = tile_tok(t)

        def emit_subtile(s, o):
            vt = vtm[s % 2]
            vt_rg = f"vtm{s % 2}"
            for h in range(2):
                bank = "a" if h == 0 else "b"
                for j in range(NCH):
                    add("pe", (lambda b_, h_, j_, o_: lambda B: B.pe.matmul(
                        PS[b_][:, 0:512],
                        lna[:, j_ * 272 + 16 + o_: j_ * 272 + 16 + o_ + 128],
                        W["wkv"][:, j_ * 1024 + h_ * 512: j_ * 1024 + (h_ + 1) * 512],
                        start=j_ == 0, stop=j_ == NCH - 1))(bank, h, j, o),
                        reads=["lna", f"w_wkv_{j}"], writes=[f"ps_{bank}"])
                add("act", (lambda b_, h_, v_: lambda B: B.act.activation(
                    out=v_[:, h_ * 512:(h_ + 1) * 512], in_=PS[b_][:, 0:512], func=AF.Copy))(bank, h, vt),
                    reads=[f"ps_{bank}"], writes=[vt_rg])
            for g in range(4):
                sbank = "s1" if g % 2 == 0 else "s2"
                for hh in range(4):
                    h = g * 4 + hh
                    i = h // 2
                    pb = (h % 2) * 64
                    add("pe", (lambda sb_, hh_, i_, pb_, o_: lambda B: B.pe.matmul(
                        PS[sb_][:, hh_ * 128:(hh_ + 1) * 128],
                        krng[pb_:pb_ + 64, i_ * 272 + 16 + o_: i_ * 272 + 16 + o_ + 128],
                        qbf[pb_:pb_ + 64, i_ * TT + o_: i_ * TT + o_ + 128],
                        start=True, stop=True))(sbank, hh, i, pb, o),
                        reads=["krng", "qbf"], writes=[f"ps_{sbank}"])
                    add("pe", (lambda h_, i_, pb_, o_: lambda B: B.pe.matmul(
                        PS["bc"][0:16, h_ * 16:(h_ + 1) * 16],
                        krng[pb_:pb_ + 64, i_ * 272 + o_: i_ * 272 + o_ + 16],
                        qbf[pb_:pb_ + 64, i_ * TT + o_: i_ * TT + o_ + 16],
                        start=True, stop=True))(h, i, pb, o),
                        reads=["krng", "qbf"], writes=["ps_bc"])
                tm = tauM[g % 2]
                tm_rg = f"tauM{g % 2}"
                add("act", (lambda sb_, tm_: lambda B: B.act.activation(
                    out=tm_[:], in_=PS[sb_][:], func=AF.Sigmoid, scale=SCALE))(sbank, tm),
                    reads=[f"ps_{sbank}"], writes=[tm_rg])
                tmv = tm.ap().rearrange("p (h w) -> p h w", h=4)
                mb = mcur.ap().unsqueeze(1).broadcast_to([P, 4, 128])
                add("dve", (lambda tmv_, mb_: lambda B: B.dve.tensor_tensor(
                    out=tmv_, in0=tmv_, in1=mb_, op=ALU.mult))(tmv, mb),
                    reads=[tm_rg, "mcur"], writes=[tm_rg])
                for hh in range(4):
                    h = g * 4 + hh
                    mbank = ("m1" if h < 8 else "a") if h % 2 == 0 else ("m2" if h < 8 else "b")
                    cs = ((h // 2) % 4) * 128
                    first = h in (0, 1, 8, 9)
                    add("pe", (lambda mb2, cs_, h_, tm_, f_, vt_, vrg: lambda B: B.pe.matmul(
                        PS[mb2][0:64, cs_:cs_ + 128],
                        vt_[:, h_ * 64:(h_ + 1) * 64],
                        tm_[:, (h_ % 4) * 128:((h_ % 4) + 1) * 128],
                        start=f_, stop=False))(mbank, cs, h, tm, first, vt, vt_rg),
                        reads=[vt_rg, tm_rg], writes=[f"ps_{mbank}"])
            add("act", lambda B: B.act.activation(
                out=tauP[:], in_=PS["bc"][0:16, 0:256], func=AF.Sigmoid, scale=SCALE),
                reads=["ps_bc"], writes=["tauP"])
            add("dve", lambda B: B.dve.tensor_tensor(
                out=tauP[:], in0=tauP[:], in1=mprev[:], op=ALU.mult),
                reads=["tauP", "mprev"], writes=["tauP"])
            for h in range(HEADS):
                mbank = ("m1" if h < 8 else "a") if h % 2 == 0 else ("m2" if h < 8 else "b")
                cs = ((h // 2) % 4) * 128
                add("pe", (lambda mb2, cs_, h_, e_: lambda B: B.pe.matmul(
                    PS[mb2][0:64, cs_:cs_ + 16],
                    vbnd[:, h_ * 64:(h_ + 1) * 64],
                    tauP[:, h_ * 16:(h_ + 1) * 16],
                    start=False, stop=e_))(mbank, cs, h, h in (6, 7, 14, 15)),
                    reads=["vbnd", "tauP"], writes=[f"ps_{mbank}"])
            for mbank, pb0, cset in (("m1", 0, 0), ("m2", 64, 0), ("a", 0, 4), ("b", 64, 4)):
                dstm = qbf.ap().rearrange("p (c w) -> p c w", c=NCH)[
                    pb0:pb0 + 64, cset:cset + 4, o:o + 128]
                srcm = PS[mbank].ap()[0:64, :].rearrange("p (c w) -> p c w", c=4)
                add("act", (lambda s_, d_: lambda B: B.act.activation(
                    out=d_, in_=s_, func=AF.Copy))(srcm, dstm),
                    reads=[f"ps_{mbank}"], writes=["qbf"])
            dma("pool", (lambda v_: lambda B: B.pool.dma_start(out=vbnd[:], in_=v_[112:128, :]))(vt),
                reads=[vt_rg], writes=["vbnd"])

        for s in range(TTt // 128):
            emit_subtile(s, s * 128)

    @_tagged
    def st_M2b(t):
        TTt = tile_tok(t)
        for i in range(NCH):
            bank, slot = proj_bank(i)
            for j in range(NCH):
                add("pe", (lambda b_, s_, i_, j_, n_: lambda B: B.pe.matmul(
                    PS[b_][:, s_:s_ + n_],
                    W["wo"][:, j_ * 1024 + i_ * 128: j_ * 1024 + (i_ + 1) * 128],
                    qbf[:, j_ * TT:j_ * TT + n_],
                    start=j_ == 0, stop=j_ == NCH - 1))(bank, slot, i, j, TTt),
                    reads=[f"w_wo_{j}", "qbf"], writes=[f"ps_{bank}"])
            add("dve", (lambda i_, b_, s_, n_: lambda B: B.dve.tensor_tensor(
                out=xr1[:, i_ * TT:i_ * TT + n_],
                in0=lna[:, i_ * 272 + 16: i_ * 272 + 16 + n_],
                in1=PS[b_][:, s_:s_ + n_], op=ALU.add))(i, bank, slot, TTt),
                reads=["lna", f"ps_{bank}"], writes=["xr1"])

    @_tagged
    def st_P2e_stats(t):
        TTt = tile_tok(t)
        yv = ch3(xr1, TT, 0, TTt)
        ln_stats(t, yv, ["xr1"], src_flat=xr1)

    @_tagged
    def st_P2e_rows(t):
        ln_rows_bcast(t)

    @_tagged
    def st_P2e_apply(t):
        TTt = tile_tok(t)
        strm = stream[t % 3]
        st_rg = f"stream{t % 3}"
        sv = ch3(strm, TT, 0, TTt)
        yv = ch3(xr1, TT, 0, TTt)
        ln_apply(t, yv, ["xr1"], ch3(dbuf, TT, 0, TTt), "dbuf")
        for j in range(NCH):
            add("dve", (lambda j_, n_: lambda B: B.dve.tensor_scalar(
                out=xk1[:, j_ * TT:j_ * TT + n_], in0=dbuf[:, j_ * TT:j_ * TT + n_],
                scalar1=cols_sb["gammas"][:, j_:j_ + 1], scalar2=cols_sb["betas"][:, j_:j_ + 1],
                op0=ALU.mult, op1=ALU.add))(j, TTt),
                reads=["dbuf", "gammas", "betas"], writes=["xk1"])
        add("dve", (lambda sv_, o3: lambda B: B.dve.tensor_tensor(
            out=sv_, in0=sv_, in1=o3, op=ALU.add))(sv, ch3(xk1, TT, 0, TTt)),
            reads=[st_rg, "xk1"], writes=[st_rg])

    @_tagged
    def st_P3(t):
        TTt = tile_tok(t)
        strm = stream[t % 3]
        st_rg = f"stream{t % 3}"
        sv = ch3(strm, TT, 0, TTt)
        if TAPS:
            dtap2 = taps["tap2"][:, :, toff(t):toff(t) + TTt]
            dma("sync", (lambda d_, s_: lambda B: B.sync.dma_start(out=d_, in_=s_))(dtap2, sv),
                reads=[st_rg])
        if t > 0:
            ring_copy(t, ln3, "ln3", 257, 1, masked=(t == 1))
        ln_stats(t, sv, [st_rg])
        ln_rows_bcast(t, bcbank="s1")
        ln_apply(t, sv, [st_rg], ch3(ln3, 257, 1, TTt), "ln3", bcbank="s1")
        ln_mixes(t, ln3, "ln3", xr3, "xr3", xk3, "xk3")

    @_tagged
    def st_M3a(t):
        ffn_proj2(t, xr3, "xr3", xk3, "xk3")
        ffn_kact(t)

    @_tagged
    def st_M3b(t):
        ffn_v(t, stream[t % 3], f"stream{t % 3}")

    @_tagged
    def st_OUT(t):
        TTt = tile_tok(t)
        sv = ch3(stream[t % 3], TT, 0, TTt)
        dsto = out_v[:, :, (t - 1) * TT:(t - 1) * TT + TTt]
        dma("sync", (lambda d_, s_: lambda B: B.sync.dma_start(out=d_, in_=s_))(dsto, sv),
            reads=[f"stream{t % 3}"])

    def wload(names):
        for nm in names:
            for j in range(NCH):
                src_ = wts_in[nm][j * 128:(j + 1) * 128, :]
                dst_ = W[nm][:, j * 1024:(j + 1) * 1024]
                dma("pool", (lambda d_, s_: lambda B: B.pool.dma_start(out=d_, in_=s_))(dst_, src_),
                    writes=[f"w_{nm}_{j}"])

    wload(("wr", "wk"))
    st_xdma(0)
    st_P1(0)
    st_M1a(0)
    wload(("wv",))
    st_M1b(0)
    wload(("wq", "wkk", "wkv"))
    dma("pool", lambda B: B.pool.dma_start(out=mcur[:], in_=mcur_in), writes=["mcur"])
    dma("pool", lambda B: B.pool.dma_start(out=mprev[:], in_=mprev_in), writes=["mprev"])
    st_P2(0)
    st_M2(0)
    wload(("wo",))
    st_xdma(1)
    st_P1(1)
    st_A2(0)
    for t in range(N_TILES):
        nxt = t + 1
        if nxt < N_TILES:
            st_M1a(nxt)
        if nxt + 1 < N_TILES:
            st_xdma(nxt + 1)
        st_M2b(t)
        st_P2e_stats(t)
        if nxt < N_TILES:
            st_M1b(nxt)
        st_P2e_rows(t)
        if nxt < N_TILES:
            st_P2(nxt)
        st_P2e_apply(t)
        st_P3(t)
        if nxt + 1 < N_TILES:
            st_P1_stats(nxt + 1)
        if nxt < N_TILES:
            st_M2(nxt)
            st_A2(nxt)
        st_M3a(t)
        if nxt + 1 < N_TILES:
            st_P1_rest(nxt + 1)
        st_M3b(t)
        if t > 0:
            st_OUT(t)

    per_eng = prog.emit_plan()

    class BB:
        def __init__(self, eng):
            self.pool = self.sync = self.pe = self.act = self.dve = eng

    with nc.Block() as block:
        for bname in ("sync", "gpsimd", "tensor", "scalar", "vector"):
            ops = per_eng[bname]

            def body(eng, ops=ops):
                B = BB(eng)
                for op in ops:
                    for semname, val in op["waits"]:
                        eng.wait_ge(sems[semname], val)
                    inst = op["fn"](B)
                    if op["inc"] is not None:
                        inst.then_inc(sems[op["inc"][0]], op["inc"][1])
            getattr(block, bname)(body)

        def closer(eng):
            for sn, tot in prog.dma_totals.items():
                if tot:
                    eng.wait_ge(sems[sn], tot)
        block.sync(closer)

    for g in reversed(ctx):
        g.__exit__(None, None, None)
    return nc


_NC_CACHE = {}


def get_program():
    if "nc" not in _NC_CACHE:
        _NC_CACHE["nc"] = build_nc()
    return _NC_CACHE["nc"]


def kernel(L0_attn_x, L0_attn_wq, L0_attn_wkv, L0_attn_wo, L0_attn_gamma, L0_attn_beta,
           L0_ffn_W_r, L0_ffn_W_k, L0_ffn_W_v, L0_ffn_mu_r, L0_ffn_mu_k):
    Bb, T, D_ = L0_attn_x.shape
    assert (Bb, T, D_) == (2, 8192, 1024)
    NCORES = 8
    x_flat = np.ascontiguousarray(np.asarray(L0_attn_x, dtype=np.float32)).reshape(Bb * T, D_)
    x_fm = np.ascontiguousarray(x_flat.T)

    wqT = np.ascontiguousarray(np.asarray(L0_attn_wq, dtype=np.float32).T)
    wkv = np.asarray(L0_attn_wkv, dtype=np.float32)
    kidx = np.array([h * 128 + j for h in range(HEADS) for j in range(HD)])
    vidx = np.array([h * 128 + HD + j for h in range(HEADS) for j in range(HD)])
    wkkT = np.ascontiguousarray(wkv[kidx, :].T)
    wkvT = np.ascontiguousarray(wkv[vidx, :].T)
    woT = np.ascontiguousarray(np.asarray(L0_attn_wo, dtype=np.float32).T)
    wrT = np.ascontiguousarray(np.asarray(L0_ffn_W_r, dtype=np.float32).T)
    wkT = np.ascontiguousarray(np.asarray(L0_ffn_W_k, dtype=np.float32).T)
    wvT = np.ascontiguousarray(np.asarray(L0_ffn_W_v, dtype=np.float32).T)

    def colsfmt(v):
        return np.ascontiguousarray(np.asarray(v, dtype=np.float32).reshape(NCH, P).T)

    murs, muks = colsfmt(L0_ffn_mu_r), colsfmt(L0_ffn_mu_k)
    gams, bets = colsfmt(L0_attn_gamma), colsfmt(L0_attn_beta)

    jj = np.arange(P)[None, :]
    uu = np.arange(P)[:, None]
    mask_cur = (((jj - uu) >= 1) & ((jj - uu) <= WIN)).astype(np.float32)
    t16 = np.arange(16)[None, :]
    u16 = np.arange(16)[:, None]
    mp = (u16 >= t16).astype(np.float32)
    mask_prev = np.ascontiguousarray(np.tile(mp, (1, HEADS)))

    in_maps = []
    for c in range(NCORES):
        seq, q = c // 4, c % 4
        s0 = q * 2048
        base = seq * 8192
        if s0 - T_HALO < 0:
            xs = np.concatenate([np.zeros((D_, T_HALO), np.float32),
                                 x_fm[:, base: base + T_OWN]], axis=1)
        else:
            xs = x_fm[:, base + s0 - T_HALO: base + s0 + T_OWN]
        in_maps.append({
            "x_sl": np.ascontiguousarray(xs),
            "wrT": wrT, "wkT": wkT, "wvT": wvT, "wqT": wqT,
            "wkkT": wkkT, "wkvT": wkvT, "woT": woT,
            "murs": murs, "muks": muks, "gammas": gams, "betas": bets,
            "mask_cur": mask_cur, "mask_prev": mask_prev,
            "bnd_mask": np.full((P, 1), 0.0 if q == 0 else 1.0, np.float32),
        })

    nc = get_program()
    res = run_bass_kernel_spmd(nc, in_maps, core_ids=list(range(NCORES)))
    out_fm = np.concatenate([np.asarray(r["out_fm"]) for r in res.results], axis=1)
    out = np.ascontiguousarray(out_fm.T).reshape(Bb, T, D_).astype(np.float32)
    return out


# version 19
# speedup vs baseline: 1.0653x; 1.0653x over previous
import numpy as np
import ml_dtypes
import concourse.bass as bass
import concourse.mybir as mybir
from concourse.bass_utils import run_bass_kernel_spmd

F32 = mybir.dt.float32
BF16 = mybir.dt.bfloat16
AF = mybir.ActivationFunctionType
ALU = mybir.AluOpType

P = 128
D = 1024
NCH = 8
HEADS = 16
HD = 64
WIN = 16
TT = 256
T_HALO = 128
N_TILES = 9
T_OWN = 2048
EPS = 1e-5
SCALE = 0.125

ENG2BLOCK = {"pe": "tensor", "act": "scalar", "dve": "vector", "pool": "gpsimd", "sync": "sync"}
TAPS = False


class Prog:
    SERIAL = False
    DMA_LANES = 4

    def __init__(self):
        self.ops = []
        self.tag = ""

    def add(self, eng, fn, reads=(), writes=()):
        self.ops.append(dict(eng=eng, fn=fn, reads=tuple(reads), writes=tuple(writes),
                             gi=len(self.ops), tag=self.tag))

    def capture(self, fn):
        saved = self.ops
        self.ops = []
        fn()
        out = self.ops
        self.ops = saved
        return out

    def interleave(self, fns):
        lists = [self.capture(f) for f in fns]
        import itertools
        for grp in itertools.zip_longest(*lists):
            for op in grp:
                if op is not None:
                    op["gi"] = len(self.ops)
                    self.ops.append(op)

    def dma(self, eng, fn, reads=(), writes=()):
        assert eng in ("sync", "pool")
        self.ops.append(dict(eng=eng, fn=fn, reads=tuple(reads), writes=tuple(writes),
                             gi=len(self.ops), dma=True, tag=self.tag))

    def resolve(self):
        if Prog.SERIAL:
            kept = [set() for _ in self.ops]
            for op in self.ops:
                if op["gi"] > 0:
                    kept[op["gi"]].add(op["gi"] - 1)
            return kept
        last_writer = {}
        readers = {}
        deps = [set() for _ in self.ops]
        for op in self.ops:
            gi = op["gi"]
            d = deps[gi]
            for r in op["reads"]:
                if r in last_writer:
                    d.add(last_writer[r])
            for w in op["writes"]:
                if w in last_writer:
                    d.add(last_writer[w])
                for rd in readers.get(w, ()):
                    d.add(rd)
            for r in op["reads"]:
                readers.setdefault(r, []).append(gi)
            for w in op["writes"]:
                last_writer[w] = gi
                readers[w] = [gi] if w in op["reads"] else []
        kept = [set() for _ in self.ops]
        for op in self.ops:
            gi = op["gi"]
            for dj in deps[gi]:
                if dj == gi:
                    continue
                pj = self.ops[dj]
                if pj.get("dma") or pj["eng"] != op["eng"] or pj.get("memset"):
                    assert dj < gi
                    kept[gi].add(dj)
        return kept

    def emit_plan(self):
        kept = self.resolve()
        needed = set()
        for ds in kept:
            needed.update(ds)
        cum = {}
        counters = {e: 0 for e in ("pe", "act", "dve", "pool_c")}
        S = Prog.DMA_LANES
        lane_cum = {(q, l): 0 for q in ("sync", "pool") for l in range(S)}
        dma_idx = {"sync": 0, "pool": 0}
        qprefix = {"sync": [], "pool": []}
        qpos = {}
        for op in self.ops:
            gi, eng = op["gi"], op["eng"]
            if op.get("dma"):
                lane = dma_idx[eng] % S
                dma_idx[eng] += 1
                lane_cum[(eng, lane)] += 16
                sem = f"dma_{eng}{lane}"
                cum[gi] = (sem, lane_cum[(eng, lane)])
                op["inc"] = (sem, 16)
                qprefix[eng].append(tuple(lane_cum[(eng, l)] for l in range(S)))
                qpos[gi] = len(qprefix[eng]) - 1
            elif gi in needed:
                key = "pool_c" if eng == "pool" else eng
                counters[key] += 1
                cum[gi] = (key, counters[key])
                op["inc"] = (key, 1)
            else:
                op["inc"] = None
        watermark = {}
        for op in self.ops:
            gi, eng = op["gi"], op["eng"]
            ws = {}
            if op.get("dma"):
                sem_self, v_self = cum[gi]
                if v_self > 16:
                    ws[sem_self] = v_self - 16
            for dj in kept[gi]:
                sem, val = cum[dj]
                if sem.startswith("dma_"):
                    q = self.ops[dj]["eng"]
                    for l, lv in enumerate(qprefix[q][qpos[dj]]):
                        s2 = f"dma_{q}{l}"
                        ws[s2] = max(ws.get(s2, 0), lv)
                else:
                    ws[sem] = max(ws.get(sem, 0), val)
            final = []
            for sem, val in ws.items():
                k = (eng, sem)
                if watermark.get(k, 0) < val:
                    watermark[k] = val
                    final.append((sem, val))
            op["waits"] = final
        per_eng = {b: [] for b in ENG2BLOCK.values()}
        for op in self.ops:
            per_eng[ENG2BLOCK[op["eng"]]].append(op)
        self.dma_totals = {f"dma_{q}{l}": lane_cum[(q, l)]
                           for q in ("sync", "pool") for l in range(S)}
        return per_eng


def tile_tok(t):
    return T_HALO if t == 0 else TT


def toff(t):
    return 0 if t == 0 else T_HALO + (t - 1) * TT


def build_nc():
    nc = bass.Bass("TRN2", target_bir_lowering=False)
    TLOC = T_HALO + T_OWN

    x_in = nc.dram_tensor("x_sl", [D, TLOC], F32, kind="ExternalInput").ap()
    wts_in = {nm: nc.dram_tensor(nm + "T", [D, D], F32, kind="ExternalInput").ap()
              for nm in ("wr", "wk", "wv", "wq", "wkk", "wkv", "wo")}
    vec_in = {nm: nc.dram_tensor(nm, [P, NCH], F32, kind="ExternalInput").ap()
              for nm in ("murs", "muks", "gammas", "betas")}
    mcur_in = nc.dram_tensor("mask_cur", [P, P], F32, kind="ExternalInput").ap()
    mprev_in = nc.dram_tensor("mask_prev", [16, 256], F32, kind="ExternalInput").ap()
    bmask_in = nc.dram_tensor("bnd_mask", [P, 1], F32, kind="ExternalInput").ap()
    out_hbm = nc.dram_tensor("out_fm", [D, T_OWN], F32, kind="ExternalOutput").ap()
    taps = {}
    if TAPS:
        for nm in ("tap1", "tap2"):
            taps[nm] = nc.dram_tensor(nm, [D, TLOC], F32, kind="ExternalOutput").ap() \
                .rearrange("(c p) t -> p c t", p=P)

    x_v = x_in.rearrange("(c p) t -> p c t", p=P)
    out_v = out_hbm.rearrange("(c p) t -> p c t", p=P)

    prog = Prog()
    ctx = []

    def sb(name, shape, dtype):
        g = nc.sbuf_tensor(name, shape, dtype)
        t = g.__enter__()
        ctx.append(g)
        return t

    def mkpsum(name):
        g = nc.psum_tensor(name, [P, 512], F32)
        t = g.__enter__()
        ctx.append(g)
        return t

    W = {nm: sb("w_" + nm, [P, NCH * 1024], BF16) for nm in
         ("wr", "wk", "wv", "wq", "wkk", "wkv", "wo")}
    stream = [sb(f"stream{i}", [P, NCH * TT], F32) for i in range(3)]
    ln1 = sb("ln1", [P, NCH * 257], BF16)
    ln3 = sb("ln3", [P, NCH * 257], BF16)
    xr1 = sb("xr1", [P, NCH * TT], BF16)
    xk1 = sb("xk1", [P, NCH * TT], BF16)
    xr3 = sb("xr3", [P, NCH * TT], BF16)
    xk3 = sb("xk3", [P, NCH * TT], BF16)
    lna = sb("lna", [P, NCH * 272], BF16)
    qbf = sb("qbf", [P, NCH * TT], BF16)
    krng = sb("krng", [P, NCH * 272], BF16)
    vtm = [sb(f"vtm{i}", [P, 1024], BF16) for i in range(2)]
    vbnd = sb("vbnd", [16, 1024], BF16)
    tauM = [sb(f"tauM{i}", [P, 512], BF16) for i in range(2)]
    tauP = sb("tauP", [16, 256], BF16)
    msgf = qbf2 = None
    mu_exp = {"murs": sb("mu_r_bf", [P, NCH], BF16),
              "muks": sb("mu_k_bf", [P, NCH], BF16)}
    pool = [sb(f"pb{i}", [P, NCH * TT], BF16) for i in range(3)]
    cpx = sb("cpx", [P, NCH * TT], BF16)
    cps = sb("cps", [P, 4 * TT], BF16)
    dbuf = sb("dbuf", [P, NCH * TT], BF16)
    rows = sb("rows", [1, 768], F32)
    mcur = sb("mcur", [P, P], BF16)
    mprev = sb("mprev", [16, 256], BF16)
    cols_sb = {nm: sb(nm + "_s", [P, NCH], F32) for nm in ("murs", "muks", "gammas", "betas")}
    bmask = sb("bmask", [P, 1], F32)
    ones1 = sb("ones1", [P, 1], BF16)
    onesk1f = sb("onesk1f", [1, P], F32)
    epsc = sb("epsc", [1, 1], F32)

    PS = {nm: mkpsum("ps_" + nm) for nm in ("a", "b", "s1", "s2", "m1", "m2", "bc", "st")}

    sems = {}
    for n in ("pe", "act", "dve", "pool_c") + tuple(
            f"dma_{q}{l}" for q in ("sync", "pool") for l in range(Prog.DMA_LANES)):
        g = nc.semaphore("sem_" + n)
        sems[n] = g.__enter__()
        ctx.append(g)

    def proj_bank(i):
        return (("a", "b", "s1", "s2")[i % 4], 256 * ((i // 4) % 2))

    def ch3(tensor, width, col0, n, c=NCH):
        v = tensor.ap().rearrange("p (c w) -> p c w", c=c)
        return v[:, :, col0:col0 + n]

    add = prog.add
    dma = prog.dma

    for nm in ("murs", "muks", "gammas", "betas"):
        dma("sync", (lambda d_, s_: lambda B: B.sync.dma_start(out=d_[:], in_=s_))(cols_sb[nm], vec_in[nm]),
            writes=[nm])
    dma("sync", lambda B: B.sync.dma_start(out=bmask[:], in_=bmask_in), writes=["bmask"])

    def add_memset(fn, writes):
        prog.add("dve", fn, (), writes)
        prog.ops[-1]["memset"] = True
    add_memset(lambda B: B.dve.memset(ones1[:], 1.0), ["ones1"])
    add_memset(lambda B: B.dve.memset(onesk1f[:], 1.0), ["onesk1"])
    add_memset(lambda B: B.dve.memset(epsc[:], EPS), ["epsc"])
    add_memset(lambda B: B.dve.memset(ch3(ln1, 257, 0, 1), 0.0), ["ln1"])
    add_memset(lambda B: B.dve.memset(ch3(ln3, 257, 0, 1), 0.0), ["ln3"])
    add_memset(lambda B: B.dve.memset(ch3(lna, 272, 0, 16), 0.0), ["lna"])
    add_memset(lambda B: B.dve.memset(ch3(krng, 272, 0, 16), 0.0), ["krng"])
    add_memset(lambda B: B.dve.memset(vbnd[:], 0.0), ["vbnd"])
    add("act", lambda B: B.act.activation(out=rows[0:1, 0:2], in_=epsc[0:1, 0:1].broadcast_to([1, 2]),
                                          func=AF.Sigmoid), reads=["epsc"], writes=["rows"])
    add("act", lambda B: B.act.activation(out=rows[0:1, 0:2], in_=epsc[0:1, 0:1].broadcast_to([1, 2]),
                                          func=AF.Sqrt), reads=["epsc"], writes=["rows"])
    for nm2 in ("a", "b", "s1", "s2", "m1", "m2", "bc", "st"):
        add("dve", (lambda n2: lambda B: B.dve.memset(PS[n2][:], 0.0))(nm2),
            writes=[f"ps_{nm2}"])
        prog.ops[-1]["memset"] = True
    for munm in ("murs", "muks"):
        dma("pool", (lambda d_, s_: lambda B: B.pool.dma_start(out=d_[:], in_=s_))(
            mu_exp[munm], vec_in[munm]), writes=["mu_" + munm])

    def _tagged(fn):
        def w(t, *a, **k):
            old = prog.tag
            prog.tag = f"{fn.__name__[3:]}({t})"
            fn(t, *a, **k)
            prog.tag = old
        w.__name__ = fn.__name__
        return w

    @_tagged
    def st_xdma(t):
        TTt = tile_tok(t)
        strm = stream[t % 3]
        dst = ch3(strm, TT, 0, TTt)
        src = x_v[:, :, toff(t):toff(t) + TTt]
        xgate = [f"xok{t - 1}"] if t > 0 else []
        dma("sync", (lambda d_, s_: lambda B: B.sync.dma_start(out=d_, in_=s_))(dst, src),
            reads=xgate, writes=[f"stream{t % 3}"])

    def ln_stats(t, src3, src_rgs, src_flat=None, first_writes=(), stbank="st"):
        TTt = tile_tok(t)
        if src_flat is None:
            for h in range(2):
                s3 = src3[:, 4 * h:4 * h + 4, :]
                xo = ch3(cpx, TT, 0, TTt)[:, 4 * h:4 * h + 4, :]
                add("act", (lambda o_, i_: lambda B: B.act.activation(
                    out=o_, in_=i_, func=AF.Copy))(xo, s3),
                    reads=src_rgs, writes=[f"pcx{h}"] + (list(first_writes) if h == 0 else []))
            sum_src, sum_rg = cpx, lambda j: [f"pcx{j // 4}"]
        else:
            sum_src, sum_rg = src_flat, lambda j: list(src_rgs)
        sq3 = ch3(cps, TT, 0, TTt, c=4)
        add("act", (lambda o_, i_: lambda B: B.act.activation(
            out=o_, in_=i_, func=AF.Square))(sq3, src3[:, 0:4, :]),
            reads=src_rgs, writes=["cps0"])
        for j in range(NCH):
            add("pe", (lambda j_, s_, e_: lambda B: B.pe.matmul(
                PS[stbank][0:1, 0:TTt], ones1[:], sum_src[:, j_ * TT:j_ * TT + TTt],
                start=s_, stop=e_))(j, j == 0, j == NCH - 1),
                reads=sum_rg(j) + ["ones1"], writes=[f"ps_{stbank}"])
        for j in range(4):
            add("pe", (lambda j_, s_: lambda B: B.pe.matmul(
                PS[stbank][0:1, 256:256 + TTt], ones1[:], cps[:, j_ * TT:j_ * TT + TTt],
                start=s_, stop=False))(j, j == 0),
                reads=["cps0", "ones1"], writes=[f"ps_{stbank}"])
        add("act", (lambda o_, i_: lambda B: B.act.activation(
            out=o_, in_=i_, func=AF.Square))(sq3, src3[:, 4:8, :]),
            reads=src_rgs, writes=["cps0", "cps1"])
        for j in range(4):
            add("pe", (lambda j_, e_: lambda B: B.pe.matmul(
                PS[stbank][0:1, 256:256 + TTt], ones1[:], cps[:, j_ * TT:j_ * TT + TTt],
                start=False, stop=e_))(j, j == 3),
                reads=["cps1", "ones1"], writes=[f"ps_{stbank}"])

    def ln_rows_bcast(t, stbank="st", bcbank="bc"):
        TTt = tile_tok(t)
        stap = PS[stbank].ap()
        add("act", (lambda s_, n_: lambda B: B.act.activation(
            out=rows[0:1, 512:512 + n_], in_=s_[0:1, 0:n_], func=AF.Square,
            scale=0.03125))(stap, TTt),
            reads=[f"ps_{stbank}"], writes=["rows"])
        add("dve", (lambda s_, n_: lambda B: B.dve.tensor_tensor(
            out=rows[0:1, 0:n_], in0=s_[0:1, 256:256 + n_],
            in1=rows[0:1, 512:512 + n_], op=ALU.subtract))(stap, TTt),
            reads=[f"ps_{stbank}", "rows"], writes=["rows"])
        add("act", (lambda n_: lambda B: B.act.activation(
            out=rows[0:1, 512:512 + n_], in_=rows[0:1, 0:n_], func=AF.Sqrt,
            scale=1.0 / D, bias=epsc[0:1, 0:1]))(TTt), reads=["rows", "epsc"], writes=["rows"])
        add("dve", (lambda n_: lambda B: B.dve.reciprocal(
            out=rows[0:1, 256:256 + n_], in_=rows[0:1, 512:512 + n_]))(TTt),
            reads=["rows"], writes=["rows"])
        add("act", (lambda s_, n_: lambda B: B.act.activation(
            out=rows[0:1, 0:n_], in_=s_[0:1, 0:n_], func=AF.Copy,
            scale=1.0 / D))(stap, TTt),
            reads=[f"ps_{stbank}", "rows"], writes=["rows"])
        add("dve", (lambda n_: lambda B: B.dve.tensor_tensor(
            out=rows[0:1, 512:512 + n_], in0=rows[0:1, 0:n_],
            in1=rows[0:1, 256:256 + n_], op=ALU.mult))(TTt),
            reads=["rows"], writes=["rows"])
        if TTt == TT:
            add("pe", lambda B: B.pe.matmul(
                PS[bcbank][:, 0:512], onesk1f[:], rows[0:1, 256:768],
                start=True, stop=True),
                reads=["rows", "onesk1"], writes=[f"ps_{bcbank}"])
        else:
            for (dst0, s_) in ((0, 256), (256, 512)):
                add("pe", (lambda d_, s2_, n_: lambda B: B.pe.matmul(
                    PS[bcbank][:, d_:d_ + n_], onesk1f[:], rows[0:1, s2_:s2_ + n_],
                    start=True, stop=True))(dst0, s_, TTt),
                    reads=["rows", "onesk1"], writes=[f"ps_{bcbank}"])

    def ln_apply(t, src3, src_rgs, dst3, dst_rg, bcbank="bc"):
        TTt = tile_tok(t)
        rb4 = PS[bcbank].ap()[:, 0:TTt].unsqueeze(1).broadcast_to([P, 4, TTt])
        mb4 = PS[bcbank].ap()[:, 256:256 + TTt].unsqueeze(1).broadcast_to([P, 4, TTt])
        for h in range(2):
            s3 = src3[:, 4 * h:4 * h + 4, :]
            d3 = dst3[:, 4 * h:4 * h + 4, :]
            add("dve", (lambda s_, d_: lambda B: B.dve.tensor_tensor(
                out=d_, in0=s_, in1=rb4, op=ALU.mult))(s3, d3),
                reads=list(src_rgs) + [f"ps_{bcbank}"], writes=[dst_rg])
            add("dve", (lambda d_: lambda B: B.dve.tensor_tensor(
                out=d_, in0=d_, in1=mb4, op=ALU.subtract))(d3),
                reads=[dst_rg, f"ps_{bcbank}"], writes=[dst_rg])

    def ring_copy(t, tensor, rg, width, halo, masked=False):
        prev_n = tile_tok(t - 1)
        src = ch3(tensor, width, prev_n, halo)
        dstc = ch3(tensor, width, 0, halo)
        if masked:
            add("dve", (lambda s_, d_: lambda B: B.dve.tensor_scalar(
                out=d_, in0=s_, scalar1=bmask[:, 0:1], scalar2=None, op0=ALU.mult))(src, dstc),
                reads=[rg, "bmask"], writes=[rg])
        else:
            add("dve", (lambda s_, d_: lambda B: B.dve.tensor_copy(out=d_, in_=s_))(src, dstc),
                reads=[rg], writes=[rg])

    def ln_mixes(t, ln_t, ln_rg, xr_t, xr_rg, xk_t, xk_rg):
        TTt = tile_tok(t)
        lnv = ch3(ln_t, 257, 1, TTt)
        lpv = ch3(ln_t, 257, 0, TTt)
        dv = ch3(dbuf, TT, 0, TTt)
        add("dve", (lambda a_, b_, d_: lambda B: B.dve.tensor_tensor(
            out=d_, in0=a_, in1=b_, op=ALU.subtract))(lnv, lpv, dv),
            reads=[ln_rg], writes=["dbuf"])
        for (xt, xrg, munm) in ((xr_t, xr_rg, "murs"), (xk_t, xk_rg, "muks")):
            for j in range(NCH):
                add("dve", (lambda j_, xt_, mu_: lambda B: B.dve.scalar_tensor_tensor(
                    out=xt_[:, j_ * TT:j_ * TT + TTt],
                    in0=dbuf[:, j_ * TT:j_ * TT + TTt],
                    scalar=mu_[:, j_:j_ + 1],
                    in1=ln_t[:, j_ * 257:j_ * 257 + TTt],
                    op0=ALU.mult, op1=ALU.add))(j, xt, mu_exp[munm].ap()),
                    reads=["dbuf", "mu_" + munm, ln_rg], writes=[xrg])

    def ffn_proj2(t, xr_t, xr_rg, xk_t, xk_rg):
        TTt = tile_tok(t)

        def proj(wname, x_t, x_rg, dstbuf, dst_rg, func, scale):
            for i in range(NCH):
                bank, slot = proj_bank(i)
                for j in range(NCH):
                    add("pe", (lambda b_, s_, i_, j_, w_, x2, n_: lambda B: B.pe.matmul(
                        PS[b_][:, s_:s_ + n_],
                        W[w_][:, j_ * 1024 + i_ * 128: j_ * 1024 + (i_ + 1) * 128],
                        x2[:, j_ * TT:j_ * TT + n_],
                        start=j_ == 0, stop=j_ == NCH - 1))(bank, slot, i, j, wname, x_t, TTt),
                        reads=[f"w_{wname}_{j}", x_rg], writes=[f"ps_{bank}"])
                add("act", (lambda i_, b_, s_, d_, f_, n_, sc_: lambda B: B.act.activation(
                    out=d_[:, i_ * TT:i_ * TT + n_], in_=PS[b_][:, s_:s_ + n_],
                    func=f_, scale=sc_))(i, bank, slot, dstbuf, func, TTt, scale),
                    reads=[f"ps_{bank}"], writes=[dst_rg])
        proj("wr", xr_t, xr_rg, pool[0], "pb0", AF.Sigmoid, 1.0)
        proj("wk", xk_t, xk_rg, pool[1], "pb1", AF.Relu, 0.7071067811865476)

    def ffn_kact(t):
        TTt = tile_tok(t)
        add("act", (lambda n_: lambda B: B.act.activation(
            out=ch3(pool[2], TT, 0, n_), in_=ch3(pool[1], TT, 0, n_), func=AF.Square))(TTt),
            reads=["pb1"], writes=["pb2"])

    def ffn_v(t, strm, st_rg):
        TTt = tile_tok(t)
        for i in range(NCH):
            bank, slot = proj_bank(i)
            for j in range(NCH):
                add("pe", (lambda b_, s_, i_, j_, n_: lambda B: B.pe.matmul(
                    PS[b_][:, s_:s_ + n_],
                    W["wv"][:, j_ * 1024 + i_ * 128: j_ * 1024 + (i_ + 1) * 128],
                    pool[2][:, j_ * TT:j_ * TT + n_],
                    start=j_ == 0, stop=j_ == NCH - 1))(bank, slot, i, j, TTt),
                    reads=["w_wv_" + str(j), "pb2"], writes=[f"ps_{bank}"])
            add("dve", (lambda i_, b_, s_, n_: lambda B: B.dve.tensor_tensor(
                out=pool[1][:, i_ * TT:i_ * TT + n_],
                in0=pool[0][:, i_ * TT:i_ * TT + n_], in1=PS[b_][:, s_:s_ + n_],
                op=ALU.mult))(i, bank, slot, TTt),
                reads=["pb0", f"ps_{bank}"], writes=["pb1"])
            add("dve", (lambda i_, st_, n_: lambda B: B.dve.tensor_tensor(
                out=st_[:, i_ * TT:i_ * TT + n_], in0=pool[1][:, i_ * TT:i_ * TT + n_],
                in1=st_[:, i_ * TT:i_ * TT + n_], op=ALU.add))(i, strm, TTt),
                reads=["pb1", st_rg], writes=[st_rg])

    @_tagged
    def st_P1_stats(t):
        TTt = tile_tok(t)
        strm = stream[t % 3]
        st_rg = f"stream{t % 3}"
        sv = ch3(strm, TT, 0, TTt)
        if t > 0:
            ring_copy(t, ln1, "ln1", 257, 1)
        ln_stats(t, sv, [st_rg], first_writes=[f"xok{t}"])

    @_tagged
    def st_P1_rest(t):
        TTt = tile_tok(t)
        strm = stream[t % 3]
        st_rg = f"stream{t % 3}"
        sv = ch3(strm, TT, 0, TTt)
        ln_rows_bcast(t)
        ln_apply(t, sv, [st_rg], ch3(ln1, 257, 1, TTt), "ln1")
        ln_mixes(t, ln1, "ln1", xr1, "xr1", xk1, "xk1")

    def st_P1(t):
        st_P1_stats(t)
        st_P1_rest(t)

    @_tagged
    def st_M1a(t):
        ffn_proj2(t, xr1, "xr1", xk1, "xk1")
        ffn_kact(t)

    @_tagged
    def st_M1b(t):
        ffn_v(t, stream[t % 3], f"stream{t % 3}")

    @_tagged
    def st_P2(t):
        TTt = tile_tok(t)
        strm = stream[t % 3]
        st_rg = f"stream{t % 3}"
        sv = ch3(strm, TT, 0, TTt)
        if t > 0:
            ring_copy(t, lna, "lna", 272, 16)
            ring_copy(t, krng, "krng", 272, 16)
        if TAPS:
            dtap = taps["tap1"][:, :, toff(t):toff(t) + TTt]
            dma("sync", (lambda d_, s_: lambda B: B.sync.dma_start(out=d_, in_=s_))(dtap, sv),
                reads=[st_rg])
        ln_stats(t, sv, [st_rg])
        ln_rows_bcast(t, bcbank="s1")
        ln_apply(t, sv, [st_rg], ch3(lna, 272, 16, TTt), "lna", bcbank="s1")

    @_tagged
    def st_M2(t):
        TTt = tile_tok(t)

        def aproj(wname, dst_t, dst_rg, width, col0):
            for i in range(NCH):
                bank, slot = proj_bank(i)
                for j in range(NCH):
                    add("pe", (lambda b_, s_, i_, j_, w_, n_: lambda B: B.pe.matmul(
                        PS[b_][:, s_:s_ + n_],
                        W[w_][:, j_ * 1024 + i_ * 128: j_ * 1024 + (i_ + 1) * 128],
                        lna[:, j_ * 272 + 16: j_ * 272 + 16 + n_],
                        start=j_ == 0, stop=j_ == NCH - 1))(bank, slot, i, j, wname, TTt),
                        reads=[f"w_{wname}_{j}", "lna"], writes=[f"ps_{bank}"])
                add("act", (lambda i_, b_, s_, d_, wd_, c0, n_: lambda B: B.act.activation(
                    out=d_[:, i_ * wd_ + c0: i_ * wd_ + c0 + n_],
                    in_=PS[b_][:, s_:s_ + n_], func=AF.Copy))(i, bank, slot, dst_t, width, col0, TTt),
                    reads=[f"ps_{bank}"], writes=[dst_rg])
        aproj("wq", qbf, "qbf", TT, 0)
        aproj("wkk", krng, "krng", 272, 16)

    @_tagged
    def st_A2(t):
        TTt = tile_tok(t)

        def emit_subtile(s, o):
            vt = vtm[s % 2]
            vt_rg = f"vtm{s % 2}"
            for h in range(2):
                bank = "a" if h == 0 else "b"
                for j in range(NCH):
                    add("pe", (lambda b_, h_, j_, o_: lambda B: B.pe.matmul(
                        PS[b_][:, 0:512],
                        lna[:, j_ * 272 + 16 + o_: j_ * 272 + 16 + o_ + 128],
                        W["wkv"][:, j_ * 1024 + h_ * 512: j_ * 1024 + (h_ + 1) * 512],
                        start=j_ == 0, stop=j_ == NCH - 1))(bank, h, j, o),
                        reads=["lna", f"w_wkv_{j}"], writes=[f"ps_{bank}"])
                add("act", (lambda b_, h_, v_: lambda B: B.act.activation(
                    out=v_[:, h_ * 512:(h_ + 1) * 512], in_=PS[b_][:, 0:512], func=AF.Copy))(bank, h, vt),
                    reads=[f"ps_{bank}"], writes=[vt_rg])
            for g in range(4):
                sbank = "s1" if g % 2 == 0 else "s2"
                for hh in range(4):
                    h = g * 4 + hh
                    i = h // 2
                    pb = (h % 2) * 64
                    add("pe", (lambda sb_, hh_, i_, pb_, o_: lambda B: B.pe.matmul(
                        PS[sb_][:, hh_ * 128:(hh_ + 1) * 128],
                        krng[pb_:pb_ + 64, i_ * 272 + 16 + o_: i_ * 272 + 16 + o_ + 128],
                        qbf[pb_:pb_ + 64, i_ * TT + o_: i_ * TT + o_ + 128],
                        start=True, stop=True))(sbank, hh, i, pb, o),
                        reads=["krng", "qbf"], writes=[f"ps_{sbank}"])
                    add("pe", (lambda h_, i_, pb_, o_: lambda B: B.pe.matmul(
                        PS["bc"][0:16, h_ * 16:(h_ + 1) * 16],
                        krng[pb_:pb_ + 64, i_ * 272 + o_: i_ * 272 + o_ + 16],
                        qbf[pb_:pb_ + 64, i_ * TT + o_: i_ * TT + o_ + 16],
                        start=True, stop=True))(h, i, pb, o),
                        reads=["krng", "qbf"], writes=["ps_bc"])
                tm = tauM[g % 2]
                tm_rg = f"tauM{g % 2}"
                add("act", (lambda sb_, tm_: lambda B: B.act.activation(
                    out=tm_[:], in_=PS[sb_][:], func=AF.Sigmoid, scale=SCALE))(sbank, tm),
                    reads=[f"ps_{sbank}"], writes=[tm_rg])
                tmv = tm.ap().rearrange("p (h w) -> p h w", h=4)
                mb = mcur.ap().unsqueeze(1).broadcast_to([P, 4, 128])
                add("dve", (lambda tmv_, mb_: lambda B: B.dve.tensor_tensor(
                    out=tmv_, in0=tmv_, in1=mb_, op=ALU.mult))(tmv, mb),
                    reads=[tm_rg, "mcur"], writes=[tm_rg])
                for hh in range(4):
                    h = g * 4 + hh
                    mbank = ("m1" if h < 8 else "a") if h % 2 == 0 else ("m2" if h < 8 else "b")
                    cs = ((h // 2) % 4) * 128
                    first = h in (0, 1, 8, 9)
                    add("pe", (lambda mb2, cs_, h_, tm_, f_, vt_, vrg: lambda B: B.pe.matmul(
                        PS[mb2][0:64, cs_:cs_ + 128],
                        vt_[:, h_ * 64:(h_ + 1) * 64],
                        tm_[:, (h_ % 4) * 128:((h_ % 4) + 1) * 128],
                        start=f_, stop=False))(mbank, cs, h, tm, first, vt, vt_rg),
                        reads=[vt_rg, tm_rg], writes=[f"ps_{mbank}"])
            add("act", lambda B: B.act.activation(
                out=tauP[:], in_=PS["bc"][0:16, 0:256], func=AF.Sigmoid, scale=SCALE),
                reads=["ps_bc"], writes=["tauP"])
            add("dve", lambda B: B.dve.tensor_tensor(
                out=tauP[:], in0=tauP[:], in1=mprev[:], op=ALU.mult),
                reads=["tauP", "mprev"], writes=["tauP"])
            for h in range(HEADS):
                mbank = ("m1" if h < 8 else "a") if h % 2 == 0 else ("m2" if h < 8 else "b")
                cs = ((h // 2) % 4) * 128
                add("pe", (lambda mb2, cs_, h_, e_: lambda B: B.pe.matmul(
                    PS[mb2][0:64, cs_:cs_ + 16],
                    vbnd[:, h_ * 64:(h_ + 1) * 64],
                    tauP[:, h_ * 16:(h_ + 1) * 16],
                    start=False, stop=e_))(mbank, cs, h, h in (6, 7, 14, 15)),
                    reads=["vbnd", "tauP"], writes=[f"ps_{mbank}"])
            for mbank, pb0, cset in (("m1", 0, 0), ("m2", 64, 0), ("a", 0, 4), ("b", 64, 4)):
                dstm = qbf.ap().rearrange("p (c w) -> p c w", c=NCH)[
                    pb0:pb0 + 64, cset:cset + 4, o:o + 128]
                srcm = PS[mbank].ap()[0:64, :].rearrange("p (c w) -> p c w", c=4)
                add("act", (lambda s_, d_: lambda B: B.act.activation(
                    out=d_, in_=s_, func=AF.Copy))(srcm, dstm),
                    reads=[f"ps_{mbank}"], writes=["qbf"])
            dma("pool", (lambda v_: lambda B: B.pool.dma_start(out=vbnd[:], in_=v_[112:128, :]))(vt),
                reads=[vt_rg], writes=["vbnd"])

        for s in range(TTt // 128):
            emit_subtile(s, s * 128)

    @_tagged
    def st_M2b(t):
        TTt = tile_tok(t)
        for i in range(NCH):
            bank, slot = proj_bank(i)
            for j in range(NCH):
                add("pe", (lambda b_, s_, i_, j_, n_: lambda B: B.pe.matmul(
                    PS[b_][:, s_:s_ + n_],
                    W["wo"][:, j_ * 1024 + i_ * 128: j_ * 1024 + (i_ + 1) * 128],
                    qbf[:, j_ * TT:j_ * TT + n_],
                    start=j_ == 0, stop=j_ == NCH - 1))(bank, slot, i, j, TTt),
                    reads=[f"w_wo_{j}", "qbf"], writes=[f"ps_{bank}"])
            add("dve", (lambda i_, b_, s_, n_: lambda B: B.dve.tensor_tensor(
                out=xr1[:, i_ * TT:i_ * TT + n_],
                in0=lna[:, i_ * 272 + 16: i_ * 272 + 16 + n_],
                in1=PS[b_][:, s_:s_ + n_], op=ALU.add))(i, bank, slot, TTt),
                reads=["lna", f"ps_{bank}"], writes=["xr1"])

    @_tagged
    def st_P2e_stats(t):
        TTt = tile_tok(t)
        yv = ch3(xr1, TT, 0, TTt)
        ln_stats(t, yv, ["xr1"], src_flat=xr1)

    @_tagged
    def st_P2e_rows(t):
        ln_rows_bcast(t)

    @_tagged
    def st_P2e_apply(t):
        TTt = tile_tok(t)
        strm = stream[t % 3]
        st_rg = f"stream{t % 3}"
        sv = ch3(strm, TT, 0, TTt)
        yv = ch3(xr1, TT, 0, TTt)
        ln_apply(t, yv, ["xr1"], ch3(dbuf, TT, 0, TTt), "dbuf")
        for j in range(NCH):
            add("dve", (lambda j_, n_: lambda B: B.dve.tensor_scalar(
                out=xk1[:, j_ * TT:j_ * TT + n_], in0=dbuf[:, j_ * TT:j_ * TT + n_],
                scalar1=cols_sb["gammas"][:, j_:j_ + 1], scalar2=cols_sb["betas"][:, j_:j_ + 1],
                op0=ALU.mult, op1=ALU.add))(j, TTt),
                reads=["dbuf", "gammas", "betas"], writes=["xk1"])
        add("dve", (lambda sv_, o3: lambda B: B.dve.tensor_tensor(
            out=sv_, in0=sv_, in1=o3, op=ALU.add))(sv, ch3(xk1, TT, 0, TTt)),
            reads=[st_rg, "xk1"], writes=[st_rg])

    @_tagged
    def st_P3(t):
        TTt = tile_tok(t)
        strm = stream[t % 3]
        st_rg = f"stream{t % 3}"
        sv = ch3(strm, TT, 0, TTt)
        if TAPS:
            dtap2 = taps["tap2"][:, :, toff(t):toff(t) + TTt]
            dma("sync", (lambda d_, s_: lambda B: B.sync.dma_start(out=d_, in_=s_))(dtap2, sv),
                reads=[st_rg])
        if t > 0:
            ring_copy(t, ln3, "ln3", 257, 1, masked=(t == 1))
        ln_stats(t, sv, [st_rg])
        ln_rows_bcast(t, bcbank="s1")
        ln_apply(t, sv, [st_rg], ch3(ln3, 257, 1, TTt), "ln3", bcbank="s1")
        ln_mixes(t, ln3, "ln3", xr3, "xr3", xk3, "xk3")

    @_tagged
    def st_M3a(t):
        ffn_proj2(t, xr3, "xr3", xk3, "xk3")
        ffn_kact(t)

    @_tagged
    def st_M3b(t):
        ffn_v(t, stream[t % 3], f"stream{t % 3}")

    @_tagged
    def st_OUT(t):
        TTt = tile_tok(t)
        sv = ch3(stream[t % 3], TT, 0, TTt)
        dsto = out_v[:, :, (t - 1) * TT:(t - 1) * TT + TTt]
        dma("sync", (lambda d_, s_: lambda B: B.sync.dma_start(out=d_, in_=s_))(dsto, sv),
            reads=[f"stream{t % 3}"])

    def wload(names):
        for nm in names:
            for j in range(NCH):
                src_ = wts_in[nm][j * 128:(j + 1) * 128, :]
                dst_ = W[nm][:, j * 1024:(j + 1) * 1024]
                dma("pool", (lambda d_, s_: lambda B: B.pool.dma_start(out=d_, in_=s_))(dst_, src_),
                    writes=[f"w_{nm}_{j}"])

    wload(("wr", "wk"))
    st_xdma(0)
    st_P1(0)
    st_M1a(0)
    wload(("wv",))
    st_M1b(0)
    wload(("wq", "wkk", "wkv"))
    dma("pool", lambda B: B.pool.dma_start(out=mcur[:], in_=mcur_in), writes=["mcur"])
    dma("pool", lambda B: B.pool.dma_start(out=mprev[:], in_=mprev_in), writes=["mprev"])
    st_P2(0)
    st_M2(0)
    wload(("wo",))
    st_xdma(1)
    st_P1(1)
    st_A2(0)
    for t in range(N_TILES):
        nxt = t + 1
        if nxt < N_TILES:
            st_M1a(nxt)
        if nxt + 1 < N_TILES:
            st_xdma(nxt + 1)
        st_M2b(t)
        st_P2e_stats(t)
        if nxt < N_TILES:
            st_M1b(nxt)
        st_P2e_rows(t)
        st_P2e_apply(t)
        if nxt < N_TILES:
            st_P2(nxt)
        st_P3(t)
        if nxt + 1 < N_TILES:
            st_P1_stats(nxt + 1)
        if nxt < N_TILES:
            st_M2(nxt)
            st_A2(nxt)
        st_M3a(t)
        if nxt + 1 < N_TILES:
            st_P1_rest(nxt + 1)
        st_M3b(t)
        if t > 0:
            st_OUT(t)

    per_eng = prog.emit_plan()

    class BB:
        def __init__(self, eng):
            self.pool = self.sync = self.pe = self.act = self.dve = eng

    with nc.Block() as block:
        for bname in ("sync", "gpsimd", "tensor", "scalar", "vector"):
            ops = per_eng[bname]

            def body(eng, ops=ops):
                B = BB(eng)
                for op in ops:
                    for semname, val in op["waits"]:
                        eng.wait_ge(sems[semname], val)
                    inst = op["fn"](B)
                    if op["inc"] is not None:
                        inst.then_inc(sems[op["inc"][0]], op["inc"][1])
            getattr(block, bname)(body)

        def closer(eng):
            for sn, tot in prog.dma_totals.items():
                if tot:
                    eng.wait_ge(sems[sn], tot)
        block.sync(closer)

    for g in reversed(ctx):
        g.__exit__(None, None, None)
    return nc


_NC_CACHE = {}


def get_program():
    if "nc" not in _NC_CACHE:
        _NC_CACHE["nc"] = build_nc()
    return _NC_CACHE["nc"]


def kernel(L0_attn_x, L0_attn_wq, L0_attn_wkv, L0_attn_wo, L0_attn_gamma, L0_attn_beta,
           L0_ffn_W_r, L0_ffn_W_k, L0_ffn_W_v, L0_ffn_mu_r, L0_ffn_mu_k):
    Bb, T, D_ = L0_attn_x.shape
    assert (Bb, T, D_) == (2, 8192, 1024)
    NCORES = 8
    x_flat = np.ascontiguousarray(np.asarray(L0_attn_x, dtype=np.float32)).reshape(Bb * T, D_)
    x_fm = np.ascontiguousarray(x_flat.T)

    wqT = np.ascontiguousarray(np.asarray(L0_attn_wq, dtype=np.float32).T)
    wkv = np.asarray(L0_attn_wkv, dtype=np.float32)
    kidx = np.array([h * 128 + j for h in range(HEADS) for j in range(HD)])
    vidx = np.array([h * 128 + HD + j for h in range(HEADS) for j in range(HD)])
    wkkT = np.ascontiguousarray(wkv[kidx, :].T)
    wkvT = np.ascontiguousarray(wkv[vidx, :].T)
    woT = np.ascontiguousarray(np.asarray(L0_attn_wo, dtype=np.float32).T)
    wrT = np.ascontiguousarray(np.asarray(L0_ffn_W_r, dtype=np.float32).T)
    wkT = np.ascontiguousarray(np.asarray(L0_ffn_W_k, dtype=np.float32).T)
    wvT = np.ascontiguousarray(np.asarray(L0_ffn_W_v, dtype=np.float32).T)

    def colsfmt(v):
        return np.ascontiguousarray(np.asarray(v, dtype=np.float32).reshape(NCH, P).T)

    murs, muks = colsfmt(L0_ffn_mu_r), colsfmt(L0_ffn_mu_k)
    gams, bets = colsfmt(L0_attn_gamma), colsfmt(L0_attn_beta)

    jj = np.arange(P)[None, :]
    uu = np.arange(P)[:, None]
    mask_cur = (((jj - uu) >= 1) & ((jj - uu) <= WIN)).astype(np.float32)
    t16 = np.arange(16)[None, :]
    u16 = np.arange(16)[:, None]
    mp = (u16 >= t16).astype(np.float32)
    mask_prev = np.ascontiguousarray(np.tile(mp, (1, HEADS)))

    in_maps = []
    for c in range(NCORES):
        seq, q = c // 4, c % 4
        s0 = q * 2048
        base = seq * 8192
        if s0 - T_HALO < 0:
            xs = np.concatenate([np.zeros((D_, T_HALO), np.float32),
                                 x_fm[:, base: base + T_OWN]], axis=1)
        else:
            xs = x_fm[:, base + s0 - T_HALO: base + s0 + T_OWN]
        in_maps.append({
            "x_sl": np.ascontiguousarray(xs),
            "wrT": wrT, "wkT": wkT, "wvT": wvT, "wqT": wqT,
            "wkkT": wkkT, "wkvT": wkvT, "woT": woT,
            "murs": murs, "muks": muks, "gammas": gams, "betas": bets,
            "mask_cur": mask_cur, "mask_prev": mask_prev,
            "bnd_mask": np.full((P, 1), 0.0 if q == 0 else 1.0, np.float32),
        })

    nc = get_program()
    res = run_bass_kernel_spmd(nc, in_maps, core_ids=list(range(NCORES)))
    out_fm = np.concatenate([np.asarray(r["out_fm"]) for r in res.results], axis=1)
    out = np.ascontiguousarray(out_fm.T).reshape(Bb, T, D_).astype(np.float32)
    return out


# version 20 (final)
# speedup vs baseline: 1.0858x; 1.0192x over previous
import numpy as np
import ml_dtypes
import concourse.bass as bass
import concourse.mybir as mybir
from concourse.bass_utils import run_bass_kernel_spmd

F32 = mybir.dt.float32
BF16 = mybir.dt.bfloat16
AF = mybir.ActivationFunctionType
ALU = mybir.AluOpType

P = 128
D = 1024
NCH = 8
HEADS = 16
HD = 64
WIN = 16
TT = 256
T_HALO = 128
N_TILES = 9
T_OWN = 2048
EPS = 1e-5
SCALE = 0.125

ENG2BLOCK = {"pe": "tensor", "act": "scalar", "dve": "vector", "pool": "gpsimd", "sync": "sync"}
TAPS = False


class Prog:
    SERIAL = False
    DMA_LANES = 4

    def __init__(self):
        self.ops = []
        self.tag = ""

    def add(self, eng, fn, reads=(), writes=()):
        self.ops.append(dict(eng=eng, fn=fn, reads=tuple(reads), writes=tuple(writes),
                             gi=len(self.ops), tag=self.tag))

    def capture(self, fn):
        saved = self.ops
        self.ops = []
        fn()
        out = self.ops
        self.ops = saved
        return out

    def interleave(self, fns):
        lists = [self.capture(f) for f in fns]
        import itertools
        for grp in itertools.zip_longest(*lists):
            for op in grp:
                if op is not None:
                    op["gi"] = len(self.ops)
                    self.ops.append(op)

    def dma(self, eng, fn, reads=(), writes=()):
        assert eng in ("sync", "pool")
        self.ops.append(dict(eng=eng, fn=fn, reads=tuple(reads), writes=tuple(writes),
                             gi=len(self.ops), dma=True, tag=self.tag))

    def resolve(self):
        if Prog.SERIAL:
            kept = [set() for _ in self.ops]
            for op in self.ops:
                if op["gi"] > 0:
                    kept[op["gi"]].add(op["gi"] - 1)
            return kept
        last_writer = {}
        readers = {}
        deps = [set() for _ in self.ops]
        for op in self.ops:
            gi = op["gi"]
            d = deps[gi]
            for r in op["reads"]:
                if r in last_writer:
                    d.add(last_writer[r])
            for w in op["writes"]:
                if w in last_writer:
                    d.add(last_writer[w])
                for rd in readers.get(w, ()):
                    d.add(rd)
            for r in op["reads"]:
                readers.setdefault(r, []).append(gi)
            for w in op["writes"]:
                last_writer[w] = gi
                readers[w] = [gi] if w in op["reads"] else []
        kept = [set() for _ in self.ops]
        for op in self.ops:
            gi = op["gi"]
            for dj in deps[gi]:
                if dj == gi:
                    continue
                pj = self.ops[dj]
                if pj.get("dma") or pj["eng"] != op["eng"] or pj.get("memset"):
                    assert dj < gi
                    kept[gi].add(dj)
        return kept

    def emit_plan(self):
        kept = self.resolve()
        needed = set()
        for ds in kept:
            needed.update(ds)
        cum = {}
        counters = {e: 0 for e in ("pe", "act", "dve", "pool_c")}
        S = Prog.DMA_LANES
        lane_cum = {(q, l): 0 for q in ("sync", "pool") for l in range(S)}
        dma_idx = {"sync": 0, "pool": 0}
        qprefix = {"sync": [], "pool": []}
        qpos = {}
        for op in self.ops:
            gi, eng = op["gi"], op["eng"]
            if op.get("dma"):
                lane = dma_idx[eng] % S
                dma_idx[eng] += 1
                lane_cum[(eng, lane)] += 16
                sem = f"dma_{eng}{lane}"
                cum[gi] = (sem, lane_cum[(eng, lane)])
                op["inc"] = (sem, 16)
                qprefix[eng].append(tuple(lane_cum[(eng, l)] for l in range(S)))
                qpos[gi] = len(qprefix[eng]) - 1
            elif gi in needed:
                key = "pool_c" if eng == "pool" else eng
                counters[key] += 1
                cum[gi] = (key, counters[key])
                op["inc"] = (key, 1)
            else:
                op["inc"] = None
        watermark = {}
        for op in self.ops:
            gi, eng = op["gi"], op["eng"]
            ws = {}
            if op.get("dma"):
                sem_self, v_self = cum[gi]
                if v_self > 16:
                    ws[sem_self] = v_self - 16
            for dj in kept[gi]:
                sem, val = cum[dj]
                if sem.startswith("dma_"):
                    q = self.ops[dj]["eng"]
                    for l, lv in enumerate(qprefix[q][qpos[dj]]):
                        s2 = f"dma_{q}{l}"
                        ws[s2] = max(ws.get(s2, 0), lv)
                else:
                    ws[sem] = max(ws.get(sem, 0), val)
            final = []
            for sem, val in ws.items():
                k = (eng, sem)
                if watermark.get(k, 0) < val:
                    watermark[k] = val
                    final.append((sem, val))
            op["waits"] = final
        per_eng = {b: [] for b in ENG2BLOCK.values()}
        for op in self.ops:
            per_eng[ENG2BLOCK[op["eng"]]].append(op)
        self.dma_totals = {f"dma_{q}{l}": lane_cum[(q, l)]
                           for q in ("sync", "pool") for l in range(S)}
        return per_eng


def tile_tok(t):
    return T_HALO if t == 0 else TT


def toff(t):
    return 0 if t == 0 else T_HALO + (t - 1) * TT


def build_nc():
    nc = bass.Bass("TRN2", target_bir_lowering=False)
    TLOC = T_HALO + T_OWN

    x_in = nc.dram_tensor("x_sl", [D, TLOC], F32, kind="ExternalInput").ap()
    wts_in = {nm: nc.dram_tensor(nm + "T", [D, D], F32, kind="ExternalInput").ap()
              for nm in ("wr", "wk", "wv", "wq", "wkk", "wkv", "wo")}
    vec_in = {nm: nc.dram_tensor(nm, [P, NCH], F32, kind="ExternalInput").ap()
              for nm in ("murs", "muks", "gammas", "betas")}
    mcur_in = nc.dram_tensor("mask_cur", [P, P], F32, kind="ExternalInput").ap()
    mprev_in = nc.dram_tensor("mask_prev", [16, 256], F32, kind="ExternalInput").ap()
    bmask_in = nc.dram_tensor("bnd_mask", [P, 1], F32, kind="ExternalInput").ap()
    out_hbm = nc.dram_tensor("out_fm", [D, T_OWN], F32, kind="ExternalOutput").ap()
    taps = {}
    if TAPS:
        for nm in ("tap1", "tap2"):
            taps[nm] = nc.dram_tensor(nm, [D, TLOC], F32, kind="ExternalOutput").ap() \
                .rearrange("(c p) t -> p c t", p=P)

    x_v = x_in.rearrange("(c p) t -> p c t", p=P)
    out_v = out_hbm.rearrange("(c p) t -> p c t", p=P)

    prog = Prog()
    ctx = []

    def sb(name, shape, dtype):
        g = nc.sbuf_tensor(name, shape, dtype)
        t = g.__enter__()
        ctx.append(g)
        return t

    def mkpsum(name):
        g = nc.psum_tensor(name, [P, 512], F32)
        t = g.__enter__()
        ctx.append(g)
        return t

    W = {nm: sb("w_" + nm, [P, NCH * 1024], BF16) for nm in
         ("wr", "wk", "wv", "wq", "wkk", "wkv", "wo")}
    stream = [sb(f"stream{i}", [P, NCH * TT], F32) for i in range(3)]
    ln1 = sb("ln1", [P, NCH * 257], BF16)
    ln3 = sb("ln3", [P, NCH * 257], BF16)
    xr1 = sb("xr1", [P, NCH * TT], BF16)
    xk1 = sb("xk1", [P, NCH * TT], BF16)
    xr3 = sb("xr3", [P, NCH * TT], BF16)
    xk3 = sb("xk3", [P, NCH * TT], BF16)
    lna = sb("lna", [P, NCH * 272], BF16)
    qbf = sb("qbf", [P, NCH * TT], BF16)
    krng = sb("krng", [P, NCH * 272], BF16)
    vtm = [sb(f"vtm{i}", [P, 1024], BF16) for i in range(2)]
    vbnd = sb("vbnd", [16, 1024], BF16)
    tauM = [sb(f"tauM{i}", [P, 512], BF16) for i in range(2)]
    tauP = sb("tauP", [16, 256], BF16)
    msgf = qbf2 = None
    mu_exp = {"murs": sb("mu_r_bf", [P, NCH], BF16),
              "muks": sb("mu_k_bf", [P, NCH], BF16)}
    pool = [sb(f"pb{i}", [P, NCH * TT], BF16) for i in range(3)]
    cpx = sb("cpx", [P, NCH * TT], BF16)
    cps = sb("cps", [P, 4 * TT], BF16)
    dbuf = sb("dbuf", [P, NCH * TT], BF16)
    rows = sb("rows", [1, 768], F32)
    mcur = sb("mcur", [P, P], BF16)
    mprev = sb("mprev", [16, 256], BF16)
    cols_sb = {nm: sb(nm + "_s", [P, NCH], F32) for nm in ("murs", "muks", "gammas", "betas")}
    bmask = sb("bmask", [P, 1], F32)
    ones1 = sb("ones1", [P, 1], BF16)
    onesk1f = sb("onesk1f", [1, P], F32)
    epsc = sb("epsc", [1, 1], F32)

    PS = {nm: mkpsum("ps_" + nm) for nm in ("a", "b", "s1", "s2", "m1", "m2", "bc", "st")}

    sems = {}
    for n in ("pe", "act", "dve", "pool_c") + tuple(
            f"dma_{q}{l}" for q in ("sync", "pool") for l in range(Prog.DMA_LANES)):
        g = nc.semaphore("sem_" + n)
        sems[n] = g.__enter__()
        ctx.append(g)

    def proj_bank(i):
        return (("a", "b", "s1", "s2")[i % 4], 256 * ((i // 4) % 2))

    def ch3(tensor, width, col0, n, c=NCH):
        v = tensor.ap().rearrange("p (c w) -> p c w", c=c)
        return v[:, :, col0:col0 + n]

    add = prog.add
    dma = prog.dma

    for nm in ("murs", "muks", "gammas", "betas"):
        dma("sync", (lambda d_, s_: lambda B: B.sync.dma_start(out=d_[:], in_=s_))(cols_sb[nm], vec_in[nm]),
            writes=[nm])
    dma("sync", lambda B: B.sync.dma_start(out=bmask[:], in_=bmask_in), writes=["bmask"])

    def add_memset(fn, writes):
        prog.add("dve", fn, (), writes)
        prog.ops[-1]["memset"] = True
    add_memset(lambda B: B.dve.memset(ones1[:], 1.0), ["ones1"])
    add_memset(lambda B: B.dve.memset(onesk1f[:], 1.0), ["onesk1"])
    add_memset(lambda B: B.dve.memset(epsc[:], EPS), ["epsc"])
    add_memset(lambda B: B.dve.memset(ch3(ln1, 257, 0, 1), 0.0), ["ln1"])
    add_memset(lambda B: B.dve.memset(ch3(ln3, 257, 0, 1), 0.0), ["ln3"])
    add_memset(lambda B: B.dve.memset(ch3(lna, 272, 0, 16), 0.0), ["lna"])
    add_memset(lambda B: B.dve.memset(ch3(krng, 272, 0, 16), 0.0), ["krng"])
    add_memset(lambda B: B.dve.memset(vbnd[:], 0.0), ["vbnd"])
    add("act", lambda B: B.act.activation(out=rows[0:1, 0:2], in_=epsc[0:1, 0:1].broadcast_to([1, 2]),
                                          func=AF.Sigmoid), reads=["epsc"], writes=["rows"])
    add("act", lambda B: B.act.activation(out=rows[0:1, 0:2], in_=epsc[0:1, 0:1].broadcast_to([1, 2]),
                                          func=AF.Sqrt), reads=["epsc"], writes=["rows"])
    for nm2 in ("a", "b", "s1", "s2", "m1", "m2", "bc", "st"):
        add("dve", (lambda n2: lambda B: B.dve.memset(PS[n2][:], 0.0))(nm2),
            writes=[f"ps_{nm2}"])
        prog.ops[-1]["memset"] = True
    for munm in ("murs", "muks"):
        dma("pool", (lambda d_, s_: lambda B: B.pool.dma_start(out=d_[:], in_=s_))(
            mu_exp[munm], vec_in[munm]), writes=["mu_" + munm])

    def _tagged(fn):
        def w(t, *a, **k):
            old = prog.tag
            prog.tag = f"{fn.__name__[3:]}({t})"
            fn(t, *a, **k)
            prog.tag = old
        w.__name__ = fn.__name__
        return w

    @_tagged
    def st_xdma(t):
        TTt = tile_tok(t)
        strm = stream[t % 3]
        dst = ch3(strm, TT, 0, TTt)
        src = x_v[:, :, toff(t):toff(t) + TTt]
        xgate = [f"xok{t - 1}"] if t > 0 else []
        dma("sync", (lambda d_, s_: lambda B: B.sync.dma_start(out=d_, in_=s_))(dst, src),
            reads=xgate, writes=[f"stream{t % 3}"])

    def ln_stats(t, src3, src_rgs, src_flat=None, first_writes=(), stbank="st"):
        TTt = tile_tok(t)
        if src_flat is None:
            for h in range(2):
                s3 = src3[:, 4 * h:4 * h + 4, :]
                xo = ch3(cpx, TT, 0, TTt)[:, 4 * h:4 * h + 4, :]
                add("act", (lambda o_, i_: lambda B: B.act.activation(
                    out=o_, in_=i_, func=AF.Copy))(xo, s3),
                    reads=src_rgs, writes=[f"pcx{h}"] + (list(first_writes) if h == 0 else []))
            sum_src, sum_rg = cpx, lambda j: [f"pcx{j // 4}"]
        else:
            sum_src, sum_rg = src_flat, lambda j: list(src_rgs)
        sq3 = ch3(cps, TT, 0, TTt, c=4)
        add("act", (lambda o_, i_: lambda B: B.act.activation(
            out=o_, in_=i_, func=AF.Square))(sq3, src3[:, 0:4, :]),
            reads=src_rgs, writes=["cps0"])
        for j in range(NCH):
            add("pe", (lambda j_, s_, e_: lambda B: B.pe.matmul(
                PS[stbank][0:1, 0:TTt], ones1[:], sum_src[:, j_ * TT:j_ * TT + TTt],
                start=s_, stop=e_))(j, j == 0, j == NCH - 1),
                reads=sum_rg(j) + ["ones1"], writes=[f"ps_{stbank}"])
        for j in range(4):
            add("pe", (lambda j_, s_: lambda B: B.pe.matmul(
                PS[stbank][0:1, 256:256 + TTt], ones1[:], cps[:, j_ * TT:j_ * TT + TTt],
                start=s_, stop=False))(j, j == 0),
                reads=["cps0", "ones1"], writes=[f"ps_{stbank}"])
        add("act", (lambda o_, i_: lambda B: B.act.activation(
            out=o_, in_=i_, func=AF.Square))(sq3, src3[:, 4:8, :]),
            reads=src_rgs, writes=["cps0", "cps1"])
        for j in range(4):
            add("pe", (lambda j_, e_: lambda B: B.pe.matmul(
                PS[stbank][0:1, 256:256 + TTt], ones1[:], cps[:, j_ * TT:j_ * TT + TTt],
                start=False, stop=e_))(j, j == 3),
                reads=["cps1", "ones1"], writes=[f"ps_{stbank}"])

    def ln_rows_bcast(t, stbank="st", bcbank="bc"):
        TTt = tile_tok(t)
        stap = PS[stbank].ap()
        add("act", (lambda s_, n_: lambda B: B.act.activation(
            out=rows[0:1, 512:512 + n_], in_=s_[0:1, 0:n_], func=AF.Square,
            scale=0.03125))(stap, TTt),
            reads=[f"ps_{stbank}"], writes=["rows"])
        add("dve", (lambda s_, n_: lambda B: B.dve.tensor_tensor(
            out=rows[0:1, 0:n_], in0=s_[0:1, 256:256 + n_],
            in1=rows[0:1, 512:512 + n_], op=ALU.subtract))(stap, TTt),
            reads=[f"ps_{stbank}", "rows"], writes=["rows"])
        add("act", (lambda n_: lambda B: B.act.activation(
            out=rows[0:1, 512:512 + n_], in_=rows[0:1, 0:n_], func=AF.Sqrt,
            scale=1.0 / D, bias=epsc[0:1, 0:1]))(TTt), reads=["rows", "epsc"], writes=["rows"])
        add("dve", (lambda n_: lambda B: B.dve.reciprocal(
            out=rows[0:1, 256:256 + n_], in_=rows[0:1, 512:512 + n_]))(TTt),
            reads=["rows"], writes=["rows"])
        add("act", (lambda s_, n_: lambda B: B.act.activation(
            out=rows[0:1, 0:n_], in_=s_[0:1, 0:n_], func=AF.Copy,
            scale=1.0 / D))(stap, TTt),
            reads=[f"ps_{stbank}", "rows"], writes=["rows"])
        add("dve", (lambda n_: lambda B: B.dve.tensor_tensor(
            out=rows[0:1, 512:512 + n_], in0=rows[0:1, 0:n_],
            in1=rows[0:1, 256:256 + n_], op=ALU.mult))(TTt),
            reads=["rows"], writes=["rows"])
        if TTt == TT:
            add("pe", lambda B: B.pe.matmul(
                PS[bcbank][:, 0:512], onesk1f[:], rows[0:1, 256:768],
                start=True, stop=True),
                reads=["rows", "onesk1"], writes=[f"ps_{bcbank}"])
        else:
            for (dst0, s_) in ((0, 256), (256, 512)):
                add("pe", (lambda d_, s2_, n_: lambda B: B.pe.matmul(
                    PS[bcbank][:, d_:d_ + n_], onesk1f[:], rows[0:1, s2_:s2_ + n_],
                    start=True, stop=True))(dst0, s_, TTt),
                    reads=["rows", "onesk1"], writes=[f"ps_{bcbank}"])

    def ln_apply(t, src3, src_rgs, dst3, dst_rg, bcbank="bc"):
        TTt = tile_tok(t)
        rb4 = PS[bcbank].ap()[:, 0:TTt].unsqueeze(1).broadcast_to([P, 4, TTt])
        mb4 = PS[bcbank].ap()[:, 256:256 + TTt].unsqueeze(1).broadcast_to([P, 4, TTt])
        for h in range(2):
            s3 = src3[:, 4 * h:4 * h + 4, :]
            d3 = dst3[:, 4 * h:4 * h + 4, :]
            add("dve", (lambda s_, d_: lambda B: B.dve.tensor_tensor(
                out=d_, in0=s_, in1=rb4, op=ALU.mult))(s3, d3),
                reads=list(src_rgs) + [f"ps_{bcbank}"], writes=[dst_rg])
            add("dve", (lambda d_: lambda B: B.dve.tensor_tensor(
                out=d_, in0=d_, in1=mb4, op=ALU.subtract))(d3),
                reads=[dst_rg, f"ps_{bcbank}"], writes=[dst_rg])

    def ring_copy(t, tensor, rg, width, halo, masked=False):
        prev_n = tile_tok(t - 1)
        src = ch3(tensor, width, prev_n, halo)
        dstc = ch3(tensor, width, 0, halo)
        if masked:
            add("dve", (lambda s_, d_: lambda B: B.dve.tensor_scalar(
                out=d_, in0=s_, scalar1=bmask[:, 0:1], scalar2=None, op0=ALU.mult))(src, dstc),
                reads=[rg, "bmask"], writes=[rg])
        else:
            add("dve", (lambda s_, d_: lambda B: B.dve.tensor_copy(out=d_, in_=s_))(src, dstc),
                reads=[rg], writes=[rg])

    def ln_mixes(t, ln_t, ln_rg, xr_t, xr_rg, xk_t, xk_rg):
        TTt = tile_tok(t)
        lnv = ch3(ln_t, 257, 1, TTt)
        lpv = ch3(ln_t, 257, 0, TTt)
        dv = ch3(dbuf, TT, 0, TTt)
        add("dve", (lambda a_, b_, d_: lambda B: B.dve.tensor_tensor(
            out=d_, in0=a_, in1=b_, op=ALU.subtract))(lnv, lpv, dv),
            reads=[ln_rg], writes=["dbuf"])
        for (xt, xrg, munm) in ((xr_t, xr_rg, "murs"), (xk_t, xk_rg, "muks")):
            for j in range(NCH):
                add("dve", (lambda j_, xt_, mu_: lambda B: B.dve.scalar_tensor_tensor(
                    out=xt_[:, j_ * TT:j_ * TT + TTt],
                    in0=dbuf[:, j_ * TT:j_ * TT + TTt],
                    scalar=mu_[:, j_:j_ + 1],
                    in1=ln_t[:, j_ * 257:j_ * 257 + TTt],
                    op0=ALU.mult, op1=ALU.add))(j, xt, mu_exp[munm].ap()),
                    reads=["dbuf", "mu_" + munm, ln_rg], writes=[xrg])

    def ffn_proj2(t, xr_t, xr_rg, xk_t, xk_rg):
        TTt = tile_tok(t)

        def proj(wname, x_t, x_rg, dstbuf, dst_rg, func, scale):
            for i in range(NCH):
                bank, slot = proj_bank(i)
                for j in range(NCH):
                    add("pe", (lambda b_, s_, i_, j_, w_, x2, n_: lambda B: B.pe.matmul(
                        PS[b_][:, s_:s_ + n_],
                        W[w_][:, j_ * 1024 + i_ * 128: j_ * 1024 + (i_ + 1) * 128],
                        x2[:, j_ * TT:j_ * TT + n_],
                        start=j_ == 0, stop=j_ == NCH - 1))(bank, slot, i, j, wname, x_t, TTt),
                        reads=[f"w_{wname}_{j}", x_rg], writes=[f"ps_{bank}"])
                add("act", (lambda i_, b_, s_, d_, f_, n_, sc_: lambda B: B.act.activation(
                    out=d_[:, i_ * TT:i_ * TT + n_], in_=PS[b_][:, s_:s_ + n_],
                    func=f_, scale=sc_))(i, bank, slot, dstbuf, func, TTt, scale),
                    reads=[f"ps_{bank}"], writes=[dst_rg])
        proj("wr", xr_t, xr_rg, pool[0], "pb0", AF.Sigmoid, 1.0)
        proj("wk", xk_t, xk_rg, pool[1], "pb1", AF.Relu, 0.7071067811865476)

    def ffn_kact(t):
        TTt = tile_tok(t)
        add("act", (lambda n_: lambda B: B.act.activation(
            out=ch3(pool[2], TT, 0, n_), in_=ch3(pool[1], TT, 0, n_), func=AF.Square))(TTt),
            reads=["pb1"], writes=["pb2"])

    def ffn_v(t, strm, st_rg, add_eng="dve"):
        TTt = tile_tok(t)
        for i in range(NCH):
            bank, slot = proj_bank(i)
            for j in range(NCH):
                add("pe", (lambda b_, s_, i_, j_, n_: lambda B: B.pe.matmul(
                    PS[b_][:, s_:s_ + n_],
                    W["wv"][:, j_ * 1024 + i_ * 128: j_ * 1024 + (i_ + 1) * 128],
                    pool[2][:, j_ * TT:j_ * TT + n_],
                    start=j_ == 0, stop=j_ == NCH - 1))(bank, slot, i, j, TTt),
                    reads=["w_wv_" + str(j), "pb2"], writes=[f"ps_{bank}"])
            add("dve", (lambda i_, b_, s_, n_: lambda B: B.dve.tensor_tensor(
                out=pool[1][:, i_ * TT:i_ * TT + n_],
                in0=pool[0][:, i_ * TT:i_ * TT + n_], in1=PS[b_][:, s_:s_ + n_],
                op=ALU.mult))(i, bank, slot, TTt),
                reads=["pb0", f"ps_{bank}"], writes=["pb1"])
            add(add_eng, (lambda i_, st_, n_: lambda B: B.dve.tensor_tensor(
                out=st_[:, i_ * TT:i_ * TT + n_], in0=pool[1][:, i_ * TT:i_ * TT + n_],
                in1=st_[:, i_ * TT:i_ * TT + n_], op=ALU.add))(i, strm, TTt),
                reads=["pb1", st_rg], writes=[st_rg])

    @_tagged
    def st_P1_stats(t):
        TTt = tile_tok(t)
        strm = stream[t % 3]
        st_rg = f"stream{t % 3}"
        sv = ch3(strm, TT, 0, TTt)
        if t > 0:
            ring_copy(t, ln1, "ln1", 257, 1)
        ln_stats(t, sv, [st_rg], first_writes=[f"xok{t}"])

    @_tagged
    def st_P1_rest(t):
        TTt = tile_tok(t)
        strm = stream[t % 3]
        st_rg = f"stream{t % 3}"
        sv = ch3(strm, TT, 0, TTt)
        ln_rows_bcast(t)
        ln_apply(t, sv, [st_rg], ch3(ln1, 257, 1, TTt), "ln1")
        ln_mixes(t, ln1, "ln1", xr1, "xr1", xk1, "xk1")

    def st_P1(t):
        st_P1_stats(t)
        st_P1_rest(t)

    @_tagged
    def st_M1a(t):
        ffn_proj2(t, xr1, "xr1", xk1, "xk1")
        ffn_kact(t)

    @_tagged
    def st_M1b(t):
        ffn_v(t, stream[t % 3], f"stream{t % 3}")

    @_tagged
    def st_P2(t):
        TTt = tile_tok(t)
        strm = stream[t % 3]
        st_rg = f"stream{t % 3}"
        sv = ch3(strm, TT, 0, TTt)
        if t > 0:
            ring_copy(t, lna, "lna", 272, 16)
            ring_copy(t, krng, "krng", 272, 16)
        if TAPS:
            dtap = taps["tap1"][:, :, toff(t):toff(t) + TTt]
            dma("sync", (lambda d_, s_: lambda B: B.sync.dma_start(out=d_, in_=s_))(dtap, sv),
                reads=[st_rg])
        ln_stats(t, sv, [st_rg])
        ln_rows_bcast(t, bcbank="s1")
        ln_apply(t, sv, [st_rg], ch3(lna, 272, 16, TTt), "lna", bcbank="s1")

    @_tagged
    def st_M2(t):
        TTt = tile_tok(t)

        def aproj(wname, dst_t, dst_rg, width, col0):
            for i in range(NCH):
                bank, slot = proj_bank(i)
                for j in range(NCH):
                    add("pe", (lambda b_, s_, i_, j_, w_, n_: lambda B: B.pe.matmul(
                        PS[b_][:, s_:s_ + n_],
                        W[w_][:, j_ * 1024 + i_ * 128: j_ * 1024 + (i_ + 1) * 128],
                        lna[:, j_ * 272 + 16: j_ * 272 + 16 + n_],
                        start=j_ == 0, stop=j_ == NCH - 1))(bank, slot, i, j, wname, TTt),
                        reads=[f"w_{wname}_{j}", "lna"], writes=[f"ps_{bank}"])
                add("act", (lambda i_, b_, s_, d_, wd_, c0, n_: lambda B: B.act.activation(
                    out=d_[:, i_ * wd_ + c0: i_ * wd_ + c0 + n_],
                    in_=PS[b_][:, s_:s_ + n_], func=AF.Copy))(i, bank, slot, dst_t, width, col0, TTt),
                    reads=[f"ps_{bank}"], writes=[dst_rg])
        aproj("wq", qbf, "qbf", TT, 0)
        aproj("wkk", krng, "krng", 272, 16)

    @_tagged
    def st_A2(t):
        TTt = tile_tok(t)

        def emit_subtile(s, o):
            vt = vtm[s % 2]
            vt_rg = f"vtm{s % 2}"
            for h in range(2):
                bank = "a" if h == 0 else "b"
                for j in range(NCH):
                    add("pe", (lambda b_, h_, j_, o_: lambda B: B.pe.matmul(
                        PS[b_][:, 0:512],
                        lna[:, j_ * 272 + 16 + o_: j_ * 272 + 16 + o_ + 128],
                        W["wkv"][:, j_ * 1024 + h_ * 512: j_ * 1024 + (h_ + 1) * 512],
                        start=j_ == 0, stop=j_ == NCH - 1))(bank, h, j, o),
                        reads=["lna", f"w_wkv_{j}"], writes=[f"ps_{bank}"])
                add("act", (lambda b_, h_, v_: lambda B: B.act.activation(
                    out=v_[:, h_ * 512:(h_ + 1) * 512], in_=PS[b_][:, 0:512], func=AF.Copy))(bank, h, vt),
                    reads=[f"ps_{bank}"], writes=[vt_rg])
            def ph_tau(g):
                sbank = "s1" if g % 2 == 0 else "s2"
                for hh in range(4):
                    h = g * 4 + hh
                    i = h // 2
                    pb = (h % 2) * 64
                    add("pe", (lambda sb_, hh_, i_, pb_, o_: lambda B: B.pe.matmul(
                        PS[sb_][:, hh_ * 128:(hh_ + 1) * 128],
                        krng[pb_:pb_ + 64, i_ * 272 + 16 + o_: i_ * 272 + 16 + o_ + 128],
                        qbf[pb_:pb_ + 64, i_ * TT + o_: i_ * TT + o_ + 128],
                        start=True, stop=True))(sbank, hh, i, pb, o),
                        reads=["krng", "qbf"], writes=[f"ps_{sbank}"])
                    add("pe", (lambda h_, i_, pb_, o_: lambda B: B.pe.matmul(
                        PS["bc"][0:16, h_ * 16:(h_ + 1) * 16],
                        krng[pb_:pb_ + 64, i_ * 272 + o_: i_ * 272 + o_ + 16],
                        qbf[pb_:pb_ + 64, i_ * TT + o_: i_ * TT + o_ + 16],
                        start=True, stop=True))(h, i, pb, o),
                        reads=["krng", "qbf"], writes=["ps_bc"])

            def ph_sig(g):
                sbank = "s1" if g % 2 == 0 else "s2"
                tm = tauM[g % 2]
                add("act", (lambda sb_, tm_: lambda B: B.act.activation(
                    out=tm_[:], in_=PS[sb_][:], func=AF.Sigmoid, scale=SCALE))(sbank, tm),
                    reads=[f"ps_{sbank}"], writes=[f"tauM{g % 2}"])

            def ph_mask(g):
                tm = tauM[g % 2]
                tmv = tm.ap().rearrange("p (h w) -> p h w", h=4)
                mb = mcur.ap().unsqueeze(1).broadcast_to([P, 4, 128])
                add("pool", (lambda tmv_, mb_: lambda B: B.pool.tensor_tensor(
                    out=tmv_, in0=tmv_, in1=mb_, op=ALU.mult))(tmv, mb),
                    reads=[f"tauM{g % 2}", "mcur"], writes=[f"tauM{g % 2}"])

            def ph_msg(g):
                tm = tauM[g % 2]
                tm_rg = f"tauM{g % 2}"
                for hh in range(4):
                    h = g * 4 + hh
                    mbank = ("m1" if h < 8 else "a") if h % 2 == 0 else ("m2" if h < 8 else "b")
                    cs = ((h // 2) % 4) * 128
                    first = h in (0, 1, 8, 9)
                    add("pe", (lambda mb2, cs_, h_, tm_, f_, vt_: lambda B: B.pe.matmul(
                        PS[mb2][0:64, cs_:cs_ + 128],
                        vt_[:, h_ * 64:(h_ + 1) * 64],
                        tm_[:, (h_ % 4) * 128:((h_ % 4) + 1) * 128],
                        start=f_, stop=False))(mbank, cs, h, tm, first, vt),
                        reads=[vt_rg, tm_rg], writes=[f"ps_{mbank}"])

            ph_tau(0); ph_tau(1); ph_sig(0); ph_mask(0)
            ph_tau(2); ph_sig(1); ph_mask(1); ph_msg(0)
            ph_tau(3); ph_sig(2); ph_mask(2); ph_msg(1)
            ph_sig(3); ph_mask(3); ph_msg(2); ph_msg(3)
            add("act", lambda B: B.act.activation(
                out=tauP[:], in_=PS["bc"][0:16, 0:256], func=AF.Sigmoid, scale=SCALE),
                reads=["ps_bc"], writes=["tauP"])
            add("pool", lambda B: B.pool.tensor_tensor(
                out=tauP[:], in0=tauP[:], in1=mprev[:], op=ALU.mult),
                reads=["tauP", "mprev"], writes=["tauP"])
            for h in range(HEADS):
                mbank = ("m1" if h < 8 else "a") if h % 2 == 0 else ("m2" if h < 8 else "b")
                cs = ((h // 2) % 4) * 128
                add("pe", (lambda mb2, cs_, h_, e_: lambda B: B.pe.matmul(
                    PS[mb2][0:64, cs_:cs_ + 16],
                    vbnd[:, h_ * 64:(h_ + 1) * 64],
                    tauP[:, h_ * 16:(h_ + 1) * 16],
                    start=False, stop=e_))(mbank, cs, h, h in (6, 7, 14, 15)),
                    reads=["vbnd", "tauP"], writes=[f"ps_{mbank}"])
            for mbank, pb0, cset in (("m1", 0, 0), ("m2", 64, 0), ("a", 0, 4), ("b", 64, 4)):
                dstm = qbf.ap().rearrange("p (c w) -> p c w", c=NCH)[
                    pb0:pb0 + 64, cset:cset + 4, o:o + 128]
                srcm = PS[mbank].ap()[0:64, :].rearrange("p (c w) -> p c w", c=4)
                add("act", (lambda s_, d_: lambda B: B.act.activation(
                    out=d_, in_=s_, func=AF.Copy))(srcm, dstm),
                    reads=[f"ps_{mbank}"], writes=["qbf"])
            dma("pool", (lambda v_: lambda B: B.pool.dma_start(out=vbnd[:], in_=v_[112:128, :]))(vt),
                reads=[vt_rg], writes=["vbnd"])

        for s in range(TTt // 128):
            emit_subtile(s, s * 128)

    @_tagged
    def st_M2b(t):
        TTt = tile_tok(t)
        for i in range(NCH):
            bank, slot = proj_bank(i)
            for j in range(NCH):
                add("pe", (lambda b_, s_, i_, j_, n_: lambda B: B.pe.matmul(
                    PS[b_][:, s_:s_ + n_],
                    W["wo"][:, j_ * 1024 + i_ * 128: j_ * 1024 + (i_ + 1) * 128],
                    qbf[:, j_ * TT:j_ * TT + n_],
                    start=j_ == 0, stop=j_ == NCH - 1))(bank, slot, i, j, TTt),
                    reads=[f"w_wo_{j}", "qbf"], writes=[f"ps_{bank}"])
            add("dve", (lambda i_, b_, s_, n_: lambda B: B.dve.tensor_tensor(
                out=xr1[:, i_ * TT:i_ * TT + n_],
                in0=lna[:, i_ * 272 + 16: i_ * 272 + 16 + n_],
                in1=PS[b_][:, s_:s_ + n_], op=ALU.add))(i, bank, slot, TTt),
                reads=["lna", f"ps_{bank}"], writes=["xr1"])

    @_tagged
    def st_P2e_stats(t):
        TTt = tile_tok(t)
        yv = ch3(xr1, TT, 0, TTt)
        ln_stats(t, yv, ["xr1"], src_flat=xr1)

    @_tagged
    def st_P2e_rows(t):
        ln_rows_bcast(t)

    @_tagged
    def st_P2e_apply(t):
        TTt = tile_tok(t)
        strm = stream[t % 3]
        st_rg = f"stream{t % 3}"
        sv = ch3(strm, TT, 0, TTt)
        yv = ch3(xr1, TT, 0, TTt)
        ln_apply(t, yv, ["xr1"], ch3(dbuf, TT, 0, TTt), "dbuf")
        for j in range(NCH):
            add("dve", (lambda j_, n_: lambda B: B.dve.tensor_scalar(
                out=xk1[:, j_ * TT:j_ * TT + n_], in0=dbuf[:, j_ * TT:j_ * TT + n_],
                scalar1=cols_sb["gammas"][:, j_:j_ + 1], scalar2=cols_sb["betas"][:, j_:j_ + 1],
                op0=ALU.mult, op1=ALU.add))(j, TTt),
                reads=["dbuf", "gammas", "betas"], writes=["xk1"])
        add("dve", (lambda sv_, o3: lambda B: B.dve.tensor_tensor(
            out=sv_, in0=sv_, in1=o3, op=ALU.add))(sv, ch3(xk1, TT, 0, TTt)),
            reads=[st_rg, "xk1"], writes=[st_rg])

    @_tagged
    def st_P3(t):
        TTt = tile_tok(t)
        strm = stream[t % 3]
        st_rg = f"stream{t % 3}"
        sv = ch3(strm, TT, 0, TTt)
        if TAPS:
            dtap2 = taps["tap2"][:, :, toff(t):toff(t) + TTt]
            dma("sync", (lambda d_, s_: lambda B: B.sync.dma_start(out=d_, in_=s_))(dtap2, sv),
                reads=[st_rg])
        if t > 0:
            ring_copy(t, ln3, "ln3", 257, 1, masked=(t == 1))
        ln_stats(t, sv, [st_rg])
        ln_rows_bcast(t, bcbank="s1")
        ln_apply(t, sv, [st_rg], ch3(ln3, 257, 1, TTt), "ln3", bcbank="s1")
        ln_mixes(t, ln3, "ln3", xr3, "xr3", xk3, "xk3")

    @_tagged
    def st_M3a(t):
        ffn_proj2(t, xr3, "xr3", xk3, "xk3")
        ffn_kact(t)

    @_tagged
    def st_M3b(t):
        ffn_v(t, stream[t % 3], f"stream{t % 3}", add_eng="pool")

    @_tagged
    def st_OUT(t):
        TTt = tile_tok(t)
        sv = ch3(stream[t % 3], TT, 0, TTt)
        dsto = out_v[:, :, (t - 1) * TT:(t - 1) * TT + TTt]
        dma("sync", (lambda d_, s_: lambda B: B.sync.dma_start(out=d_, in_=s_))(dsto, sv),
            reads=[f"stream{t % 3}"])

    def wload(names):
        for nm in names:
            for j in range(NCH):
                src_ = wts_in[nm][j * 128:(j + 1) * 128, :]
                dst_ = W[nm][:, j * 1024:(j + 1) * 1024]
                dma("pool", (lambda d_, s_: lambda B: B.pool.dma_start(out=d_, in_=s_))(dst_, src_),
                    writes=[f"w_{nm}_{j}"])

    wload(("wr", "wk"))
    st_xdma(0)
    st_P1(0)
    st_M1a(0)
    wload(("wv",))
    st_M1b(0)
    wload(("wq", "wkk", "wkv"))
    dma("pool", lambda B: B.pool.dma_start(out=mcur[:], in_=mcur_in), writes=["mcur"])
    dma("pool", lambda B: B.pool.dma_start(out=mprev[:], in_=mprev_in), writes=["mprev"])
    st_P2(0)
    st_M2(0)
    wload(("wo",))
    st_xdma(1)
    st_P1(1)
    st_A2(0)
    for t in range(N_TILES):
        nxt = t + 1
        if nxt < N_TILES:
            st_M1a(nxt)
        if nxt + 1 < N_TILES:
            st_xdma(nxt + 1)
        st_M2b(t)
        st_P2e_stats(t)
        if nxt < N_TILES:
            st_M1b(nxt)
        st_P2e_rows(t)
        st_P2e_apply(t)
        if nxt < N_TILES:
            st_P2(nxt)
        st_P3(t)
        if nxt + 1 < N_TILES:
            st_P1_stats(nxt + 1)
        if nxt < N_TILES:
            st_M2(nxt)
            st_A2(nxt)
        st_M3a(t)
        st_M3b(t)
        if nxt + 1 < N_TILES:
            st_P1_rest(nxt + 1)
        if t > 0:
            st_OUT(t)

    per_eng = prog.emit_plan()

    class BB:
        def __init__(self, eng):
            self.pool = self.sync = self.pe = self.act = self.dve = eng

    with nc.Block() as block:
        for bname in ("sync", "gpsimd", "tensor", "scalar", "vector"):
            ops = per_eng[bname]

            def body(eng, ops=ops):
                B = BB(eng)
                for op in ops:
                    for semname, val in op["waits"]:
                        eng.wait_ge(sems[semname], val)
                    inst = op["fn"](B)
                    if op["inc"] is not None:
                        inst.then_inc(sems[op["inc"][0]], op["inc"][1])
            getattr(block, bname)(body)

        def closer(eng):
            for sn, tot in prog.dma_totals.items():
                if tot:
                    eng.wait_ge(sems[sn], tot)
        block.sync(closer)

    for g in reversed(ctx):
        g.__exit__(None, None, None)
    return nc


_NC_CACHE = {}


def get_program():
    if "nc" not in _NC_CACHE:
        _NC_CACHE["nc"] = build_nc()
    return _NC_CACHE["nc"]


def kernel(L0_attn_x, L0_attn_wq, L0_attn_wkv, L0_attn_wo, L0_attn_gamma, L0_attn_beta,
           L0_ffn_W_r, L0_ffn_W_k, L0_ffn_W_v, L0_ffn_mu_r, L0_ffn_mu_k):
    Bb, T, D_ = L0_attn_x.shape
    assert (Bb, T, D_) == (2, 8192, 1024)
    NCORES = 8
    x_flat = np.ascontiguousarray(np.asarray(L0_attn_x, dtype=np.float32)).reshape(Bb * T, D_)
    x_fm = np.ascontiguousarray(x_flat.T)

    wqT = np.ascontiguousarray(np.asarray(L0_attn_wq, dtype=np.float32).T)
    wkv = np.asarray(L0_attn_wkv, dtype=np.float32)
    kidx = np.array([h * 128 + j for h in range(HEADS) for j in range(HD)])
    vidx = np.array([h * 128 + HD + j for h in range(HEADS) for j in range(HD)])
    wkkT = np.ascontiguousarray(wkv[kidx, :].T)
    wkvT = np.ascontiguousarray(wkv[vidx, :].T)
    woT = np.ascontiguousarray(np.asarray(L0_attn_wo, dtype=np.float32).T)
    wrT = np.ascontiguousarray(np.asarray(L0_ffn_W_r, dtype=np.float32).T)
    wkT = np.ascontiguousarray(np.asarray(L0_ffn_W_k, dtype=np.float32).T)
    wvT = np.ascontiguousarray(np.asarray(L0_ffn_W_v, dtype=np.float32).T)

    def colsfmt(v):
        return np.ascontiguousarray(np.asarray(v, dtype=np.float32).reshape(NCH, P).T)

    murs, muks = colsfmt(L0_ffn_mu_r), colsfmt(L0_ffn_mu_k)
    gams, bets = colsfmt(L0_attn_gamma), colsfmt(L0_attn_beta)

    jj = np.arange(P)[None, :]
    uu = np.arange(P)[:, None]
    mask_cur = (((jj - uu) >= 1) & ((jj - uu) <= WIN)).astype(np.float32)
    t16 = np.arange(16)[None, :]
    u16 = np.arange(16)[:, None]
    mp = (u16 >= t16).astype(np.float32)
    mask_prev = np.ascontiguousarray(np.tile(mp, (1, HEADS)))

    in_maps = []
    for c in range(NCORES):
        seq, q = c // 4, c % 4
        s0 = q * 2048
        base = seq * 8192
        if s0 - T_HALO < 0:
            xs = np.concatenate([np.zeros((D_, T_HALO), np.float32),
                                 x_fm[:, base: base + T_OWN]], axis=1)
        else:
            xs = x_fm[:, base + s0 - T_HALO: base + s0 + T_OWN]
        in_maps.append({
            "x_sl": np.ascontiguousarray(xs),
            "wrT": wrT, "wkT": wkT, "wvT": wvT, "wqT": wqT,
            "wkkT": wkkT, "wkvT": wkvT, "woT": woT,
            "murs": murs, "muks": muks, "gammas": gams, "betas": bets,
            "mask_cur": mask_cur, "mask_prev": mask_prev,
            "bnd_mask": np.full((P, 1), 0.0 if q == 0 else 1.0, np.float32),
        })

    nc = get_program()
    res = run_bass_kernel_spmd(nc, in_maps, core_ids=list(range(NCORES)))
    out_fm = np.concatenate([np.asarray(r["out_fm"]) for r in res.results], axis=1)
    out = np.ascontiguousarray(out_fm.T).reshape(Bb, T, D_).astype(np.float32)
    return out


# version 21
# speedup vs baseline: 1.1097x; 1.0220x over previous
import numpy as np
import ml_dtypes
import concourse.bass as bass
import concourse.mybir as mybir
from concourse.bass_utils import run_bass_kernel_spmd

F32 = mybir.dt.float32
BF16 = mybir.dt.bfloat16
AF = mybir.ActivationFunctionType
ALU = mybir.AluOpType

P = 128
D = 1024
NCH = 8
HEADS = 16
HD = 64
WIN = 16
TT = 256
T_HALO = 128
N_TILES = 9
T_OWN = 2048
EPS = 1e-5
SCALE = 0.125

ENG2BLOCK = {"pe": "tensor", "act": "scalar", "dve": "vector", "pool": "gpsimd", "sync": "sync"}
TAPS = False


class Prog:
    SERIAL = False
    DMA_LANES = 4

    def __init__(self):
        self.ops = []
        self.tag = ""

    def add(self, eng, fn, reads=(), writes=()):
        self.ops.append(dict(eng=eng, fn=fn, reads=tuple(reads), writes=tuple(writes),
                             gi=len(self.ops), tag=self.tag))

    def capture(self, fn):
        saved = self.ops
        self.ops = []
        fn()
        out = self.ops
        self.ops = saved
        return out

    def interleave(self, fns):
        lists = [self.capture(f) for f in fns]
        import itertools
        for grp in itertools.zip_longest(*lists):
            for op in grp:
                if op is not None:
                    op["gi"] = len(self.ops)
                    self.ops.append(op)

    def dma(self, eng, fn, reads=(), writes=()):
        assert eng in ("sync", "pool")
        self.ops.append(dict(eng=eng, fn=fn, reads=tuple(reads), writes=tuple(writes),
                             gi=len(self.ops), dma=True, tag=self.tag))

    def resolve(self):
        if Prog.SERIAL:
            kept = [set() for _ in self.ops]
            for op in self.ops:
                if op["gi"] > 0:
                    kept[op["gi"]].add(op["gi"] - 1)
            return kept
        last_writer = {}
        readers = {}
        deps = [set() for _ in self.ops]
        for op in self.ops:
            gi = op["gi"]
            d = deps[gi]
            for r in op["reads"]:
                if r in last_writer:
                    d.add(last_writer[r])
            for w in op["writes"]:
                if w in last_writer:
                    d.add(last_writer[w])
                for rd in readers.get(w, ()):
                    d.add(rd)
            for r in op["reads"]:
                readers.setdefault(r, []).append(gi)
            for w in op["writes"]:
                last_writer[w] = gi
                readers[w] = [gi] if w in op["reads"] else []
        kept = [set() for _ in self.ops]
        for op in self.ops:
            gi = op["gi"]
            for dj in deps[gi]:
                if dj == gi:
                    continue
                pj = self.ops[dj]
                if pj.get("dma") or pj["eng"] != op["eng"] or pj.get("memset"):
                    assert dj < gi
                    kept[gi].add(dj)
        return kept

    def emit_plan(self):
        kept = self.resolve()
        needed = set()
        for ds in kept:
            needed.update(ds)
        cum = {}
        counters = {e: 0 for e in ("pe", "act", "dve", "pool_c")}
        S = Prog.DMA_LANES
        lane_cum = {(q, l): 0 for q in ("sync", "pool") for l in range(S)}
        dma_idx = {"sync": 0, "pool": 0}
        qprefix = {"sync": [], "pool": []}
        qpos = {}
        for op in self.ops:
            gi, eng = op["gi"], op["eng"]
            if op.get("dma"):
                lane = dma_idx[eng] % S
                dma_idx[eng] += 1
                lane_cum[(eng, lane)] += 16
                sem = f"dma_{eng}{lane}"
                cum[gi] = (sem, lane_cum[(eng, lane)])
                op["inc"] = (sem, 16)
                qprefix[eng].append(tuple(lane_cum[(eng, l)] for l in range(S)))
                qpos[gi] = len(qprefix[eng]) - 1
            elif gi in needed:
                key = "pool_c" if eng == "pool" else eng
                counters[key] += 1
                cum[gi] = (key, counters[key])
                op["inc"] = (key, 1)
            else:
                op["inc"] = None
        watermark = {}
        for op in self.ops:
            gi, eng = op["gi"], op["eng"]
            ws = {}
            if op.get("dma"):
                sem_self, v_self = cum[gi]
                if v_self > 16:
                    ws[sem_self] = v_self - 16
            for dj in kept[gi]:
                sem, val = cum[dj]
                if sem.startswith("dma_"):
                    q = self.ops[dj]["eng"]
                    for l, lv in enumerate(qprefix[q][qpos[dj]]):
                        s2 = f"dma_{q}{l}"
                        ws[s2] = max(ws.get(s2, 0), lv)
                else:
                    ws[sem] = max(ws.get(sem, 0), val)
            final = []
            for sem, val in ws.items():
                k = (eng, sem)
                if watermark.get(k, 0) < val:
                    watermark[k] = val
                    final.append((sem, val))
            op["waits"] = final
        per_eng = {b: [] for b in ENG2BLOCK.values()}
        for op in self.ops:
            per_eng[ENG2BLOCK[op["eng"]]].append(op)
        self.dma_totals = {f"dma_{q}{l}": lane_cum[(q, l)]
                           for q in ("sync", "pool") for l in range(S)}
        return per_eng


def tile_tok(t):
    return T_HALO if t == 0 else TT


def toff(t):
    return 0 if t == 0 else T_HALO + (t - 1) * TT


def build_nc():
    nc = bass.Bass("TRN2", target_bir_lowering=False)
    TLOC = T_HALO + T_OWN

    x_in = nc.dram_tensor("x_sl", [D, TLOC], F32, kind="ExternalInput").ap()
    wts_in = {nm: nc.dram_tensor(nm + "T", [D, D], F32, kind="ExternalInput").ap()
              for nm in ("wr", "wk", "wv", "wq", "wkk", "wkv", "wo")}
    vec_in = {nm: nc.dram_tensor(nm, [P, NCH], F32, kind="ExternalInput").ap()
              for nm in ("murs", "muks", "gammas", "betas")}
    mcur_in = nc.dram_tensor("mask_cur", [P, P], F32, kind="ExternalInput").ap()
    mprev_in = nc.dram_tensor("mask_prev", [16, 256], F32, kind="ExternalInput").ap()
    bmask_in = nc.dram_tensor("bnd_mask", [P, 1], F32, kind="ExternalInput").ap()
    out_hbm = nc.dram_tensor("out_fm", [D, T_OWN], F32, kind="ExternalOutput").ap()
    taps = {}
    if TAPS:
        for nm in ("tap1", "tap2"):
            taps[nm] = nc.dram_tensor(nm, [D, TLOC], F32, kind="ExternalOutput").ap() \
                .rearrange("(c p) t -> p c t", p=P)

    x_v = x_in.rearrange("(c p) t -> p c t", p=P)
    out_v = out_hbm.rearrange("(c p) t -> p c t", p=P)

    prog = Prog()
    ctx = []

    def sb(name, shape, dtype):
        g = nc.sbuf_tensor(name, shape, dtype)
        t = g.__enter__()
        ctx.append(g)
        return t

    def mkpsum(name):
        g = nc.psum_tensor(name, [P, 512], F32)
        t = g.__enter__()
        ctx.append(g)
        return t

    W = {nm: sb("w_" + nm, [P, NCH * 1024], BF16) for nm in
         ("wr", "wk", "wv", "wq", "wkk", "wkv", "wo")}
    stream = [sb(f"stream{i}", [P, NCH * TT], F32) for i in range(3)]
    ln1 = sb("ln1", [P, NCH * 257], BF16)
    ln3 = sb("ln3", [P, NCH * 257], BF16)
    xr1 = sb("xr1", [P, NCH * TT], BF16)
    xk1 = sb("xk1", [P, NCH * TT], BF16)
    xr3 = sb("xr3", [P, NCH * TT], BF16)
    xk3 = sb("xk3", [P, NCH * TT], BF16)
    lna = sb("lna", [P, NCH * 272], BF16)
    qbf = sb("qbf", [P, NCH * TT], BF16)
    krng = sb("krng", [P, NCH * 272], BF16)
    vtm = [sb(f"vtm{i}", [P, 1024], BF16) for i in range(2)]
    vbnd = sb("vbnd", [16, 1024], BF16)
    tauM = [sb(f"tauM{i}", [P, 512], BF16) for i in range(2)]
    tauP = sb("tauP", [16, 256], BF16)
    msgf = qbf2 = None
    mu_exp = {"murs": sb("mu_r_bf", [P, NCH], BF16),
              "muks": sb("mu_k_bf", [P, NCH], BF16)}
    pool = [sb(f"pb{i}", [P, NCH * TT], BF16) for i in range(3)]
    cpx = sb("cpx", [P, NCH * TT], BF16)
    cps = sb("cps", [P, 4 * TT], BF16)
    dbuf = sb("dbuf", [P, NCH * TT], BF16)
    rows4 = sb("rows4", [33, 768], F32)
    rowsb = {0: rows4.ap()[0:1, :], 1: rows4.ap()[32:33, :]}
    rows = rowsb[0]
    mcur = sb("mcur", [P, P], BF16)
    mprev = sb("mprev", [16, 256], BF16)
    cols_sb = {nm: sb(nm + "_s", [P, NCH], F32) for nm in ("murs", "muks", "gammas", "betas")}
    bmask = sb("bmask", [P, 1], F32)
    ones1 = sb("ones1", [P, 1], BF16)
    onesk1f_t = sb("onesk1f", [33, P], F32)
    onesk1f = onesk1f_t.ap()[0:1, :]
    onesk1f32 = onesk1f_t.ap()[32:33, :]
    epsc = sb("epsc", [1, 1], F32)

    PS = {nm: mkpsum("ps_" + nm) for nm in ("a", "b", "s1", "s2", "m1", "m2", "bc", "st")}

    sems = {}
    for n in ("pe", "act", "dve", "pool_c") + tuple(
            f"dma_{q}{l}" for q in ("sync", "pool") for l in range(Prog.DMA_LANES)):
        g = nc.semaphore("sem_" + n)
        sems[n] = g.__enter__()
        ctx.append(g)

    def proj_bank(i):
        return (("a", "b", "s1", "s2")[i % 4], 256 * ((i // 4) % 2))

    def ch3(tensor, width, col0, n, c=NCH):
        v = tensor.ap().rearrange("p (c w) -> p c w", c=c)
        return v[:, :, col0:col0 + n]

    add = prog.add
    dma = prog.dma

    for nm in ("murs", "muks", "gammas", "betas"):
        dma("sync", (lambda d_, s_: lambda B: B.sync.dma_start(out=d_[:], in_=s_))(cols_sb[nm], vec_in[nm]),
            writes=[nm])
    dma("sync", lambda B: B.sync.dma_start(out=bmask[:], in_=bmask_in), writes=["bmask"])

    def add_memset(fn, writes):
        prog.add("dve", fn, (), writes)
        prog.ops[-1]["memset"] = True
    add_memset(lambda B: B.dve.memset(ones1[:], 1.0), ["ones1"])
    add_memset(lambda B: B.dve.memset(onesk1f_t[:], 1.0), ["onesk1"])
    add_memset(lambda B: B.dve.memset(epsc[:], EPS), ["epsc"])
    add_memset(lambda B: B.dve.memset(ch3(ln1, 257, 0, 1), 0.0), ["ln1"])
    add_memset(lambda B: B.dve.memset(ch3(ln3, 257, 0, 1), 0.0), ["ln3"])
    add_memset(lambda B: B.dve.memset(ch3(lna, 272, 0, 16), 0.0), ["lna"])
    add_memset(lambda B: B.dve.memset(ch3(krng, 272, 0, 16), 0.0), ["krng"])
    add_memset(lambda B: B.dve.memset(vbnd[:], 0.0), ["vbnd"])
    add("act", lambda B: B.act.activation(out=rows[0:1, 0:2], in_=epsc[0:1, 0:1].broadcast_to([1, 2]),
                                          func=AF.Sigmoid), reads=["epsc"], writes=["rows"])
    add("act", lambda B: B.act.activation(out=rows[0:1, 0:2], in_=epsc[0:1, 0:1].broadcast_to([1, 2]),
                                          func=AF.Sqrt), reads=["epsc"], writes=["rows"])
    for nm2 in ("a", "b", "s1", "s2", "m1", "m2", "bc", "st"):
        add("dve", (lambda n2: lambda B: B.dve.memset(PS[n2][:], 0.0))(nm2),
            writes=[f"ps_{nm2}"])
        prog.ops[-1]["memset"] = True
    for munm in ("murs", "muks"):
        dma("pool", (lambda d_, s_: lambda B: B.pool.dma_start(out=d_[:], in_=s_))(
            mu_exp[munm], vec_in[munm]), writes=["mu_" + munm])

    def _tagged(fn):
        def w(t, *a, **k):
            old = prog.tag
            prog.tag = f"{fn.__name__[3:]}({t})"
            fn(t, *a, **k)
            prog.tag = old
        w.__name__ = fn.__name__
        return w

    @_tagged
    def st_xdma(t):
        TTt = tile_tok(t)
        strm = stream[t % 3]
        dst = ch3(strm, TT, 0, TTt)
        src = x_v[:, :, toff(t):toff(t) + TTt]
        xgate = [f"xok{t - 1}"] if t > 0 else []
        dma("sync", (lambda d_, s_: lambda B: B.sync.dma_start(out=d_, in_=s_))(dst, src),
            reads=xgate, writes=[f"stream{t % 3}"])

    def ln_stats(t, src3, src_rgs, src_flat=None, first_writes=(), stbank="st", strow=0, rslot=0):
        TTt = tile_tok(t)
        if src_flat is None:
            for h in range(2):
                s3 = src3[:, 4 * h:4 * h + 4, :]
                xo = ch3(cpx, TT, 0, TTt)[:, 4 * h:4 * h + 4, :]
                add("act", (lambda o_, i_: lambda B: B.act.activation(
                    out=o_, in_=i_, func=AF.Copy))(xo, s3),
                    reads=src_rgs, writes=[f"pcx{h}"] + (list(first_writes) if h == 0 else []))
            sum_src, sum_rg = cpx, lambda j: [f"pcx{j // 4}"]
        else:
            sum_src, sum_rg = src_flat, lambda j: list(src_rgs)
        sq3 = ch3(cps, TT, 0, TTt, c=4)
        add("act", (lambda o_, i_: lambda B: B.act.activation(
            out=o_, in_=i_, func=AF.Square))(sq3, src3[:, 0:4, :]),
            reads=src_rgs, writes=["cps0"])
        rg = f"ps_{stbank}_r{strow}"
        del rslot
        for j in range(NCH):
            add("pe", (lambda j_, s_, e_: lambda B: B.pe.matmul(
                PS[stbank][strow:strow + 1, 0:TTt], ones1[:], sum_src[:, j_ * TT:j_ * TT + TTt],
                start=s_, stop=e_))(j, j == 0, j == NCH - 1),
                reads=sum_rg(j) + ["ones1"], writes=[rg])
        for j in range(4):
            add("pe", (lambda j_, s_: lambda B: B.pe.matmul(
                PS[stbank][strow:strow + 1, 256:256 + TTt], ones1[:], cps[:, j_ * TT:j_ * TT + TTt],
                start=s_, stop=False))(j, j == 0),
                reads=["cps0", "ones1"], writes=[rg])
        add("act", (lambda o_, i_: lambda B: B.act.activation(
            out=o_, in_=i_, func=AF.Square))(sq3, src3[:, 4:8, :]),
            reads=src_rgs, writes=["cps0", "cps1"])
        for j in range(4):
            add("pe", (lambda j_, e_: lambda B: B.pe.matmul(
                PS[stbank][strow:strow + 1, 256:256 + TTt], ones1[:], cps[:, j_ * TT:j_ * TT + TTt],
                start=False, stop=e_))(j, j == 3),
                reads=["cps1", "ones1"], writes=[rg])

    def ln_rows_bcast(t, stbank="st", bcbank="bc", strow=0, rslot=0):
        TTt = tile_tok(t)
        stap = PS[stbank].ap()
        rg = f"ps_{stbank}_r{strow}"
        rw = rowsb[rslot]
        rw_rg = f"rows{rslot}"
        add("act", (lambda s_, n_: lambda B: B.act.activation(
            out=rw[0:1, 512:512 + n_], in_=s_[strow:strow + 1, 0:n_], func=AF.Square,
            scale=0.03125))(stap, TTt),
            reads=[rg], writes=[rw_rg])
        add("dve", (lambda s_, n_: lambda B: B.dve.tensor_tensor(
            out=rw[0:1, 0:n_], in0=s_[strow:strow + 1, 256:256 + n_],
            in1=rw[0:1, 512:512 + n_], op=ALU.subtract))(stap, TTt),
            reads=[rg, rw_rg], writes=[rw_rg])
        add("act", (lambda n_: lambda B: B.act.activation(
            out=rw[0:1, 512:512 + n_], in_=rw[0:1, 0:n_], func=AF.Sqrt,
            scale=1.0 / D, bias=epsc[0:1, 0:1]))(TTt), reads=[rw_rg, "epsc"], writes=[rw_rg])
        add("dve", (lambda n_: lambda B: B.dve.reciprocal(
            out=rw[0:1, 256:256 + n_], in_=rw[0:1, 512:512 + n_]))(TTt),
            reads=[rw_rg], writes=[rw_rg])
        add("act", (lambda s_, n_: lambda B: B.act.activation(
            out=rw[0:1, 0:n_], in_=s_[strow:strow + 1, 0:n_], func=AF.Copy,
            scale=1.0 / D))(stap, TTt),
            reads=[rg, rw_rg], writes=[rw_rg])
        add("dve", (lambda n_: lambda B: B.dve.tensor_tensor(
            out=rw[0:1, 512:512 + n_], in0=rw[0:1, 0:n_],
            in1=rw[0:1, 256:256 + n_], op=ALU.mult))(TTt),
            reads=[rw_rg], writes=[rw_rg])
        onesr = onesk1f if rslot == 0 else onesk1f32
        if TTt == TT:
            add("pe", (lambda o1: lambda B: B.pe.matmul(
                PS[bcbank][:, 0:512], o1, rw[0:1, 256:768],
                start=True, stop=True))(onesr),
                reads=[rw_rg, "onesk1"], writes=[f"ps_{bcbank}"])
        else:
            for (dst0, s_) in ((0, 256), (256, 512)):
                add("pe", (lambda d_, s2_, n_, o1: lambda B: B.pe.matmul(
                    PS[bcbank][:, d_:d_ + n_], o1, rw[0:1, s2_:s2_ + n_],
                    start=True, stop=True))(dst0, s_, TTt, onesr),
                    reads=[rw_rg, "onesk1"], writes=[f"ps_{bcbank}"])

    def ln_apply(t, src3, src_rgs, dst3, dst_rg, bcbank="bc"):
        TTt = tile_tok(t)
        rb4 = PS[bcbank].ap()[:, 0:TTt].unsqueeze(1).broadcast_to([P, 4, TTt])
        mb4 = PS[bcbank].ap()[:, 256:256 + TTt].unsqueeze(1).broadcast_to([P, 4, TTt])
        for h in range(2):
            s3 = src3[:, 4 * h:4 * h + 4, :]
            d3 = dst3[:, 4 * h:4 * h + 4, :]
            add("dve", (lambda s_, d_: lambda B: B.dve.tensor_tensor(
                out=d_, in0=s_, in1=rb4, op=ALU.mult))(s3, d3),
                reads=list(src_rgs) + [f"ps_{bcbank}"], writes=[dst_rg])
            add("dve", (lambda d_: lambda B: B.dve.tensor_tensor(
                out=d_, in0=d_, in1=mb4, op=ALU.subtract))(d3),
                reads=[dst_rg, f"ps_{bcbank}"], writes=[dst_rg])

    def ring_copy(t, tensor, rg, width, halo, masked=False):
        prev_n = tile_tok(t - 1)
        src = ch3(tensor, width, prev_n, halo)
        dstc = ch3(tensor, width, 0, halo)
        if masked:
            add("dve", (lambda s_, d_: lambda B: B.dve.tensor_scalar(
                out=d_, in0=s_, scalar1=bmask[:, 0:1], scalar2=None, op0=ALU.mult))(src, dstc),
                reads=[rg, "bmask"], writes=[rg])
        else:
            add("dve", (lambda s_, d_: lambda B: B.dve.tensor_copy(out=d_, in_=s_))(src, dstc),
                reads=[rg], writes=[rg])

    def ln_mixes(t, ln_t, ln_rg, xr_t, xr_rg, xk_t, xk_rg):
        TTt = tile_tok(t)
        lnv = ch3(ln_t, 257, 1, TTt)
        lpv = ch3(ln_t, 257, 0, TTt)
        dv = ch3(dbuf, TT, 0, TTt)
        add("dve", (lambda a_, b_, d_: lambda B: B.dve.tensor_tensor(
            out=d_, in0=a_, in1=b_, op=ALU.subtract))(lnv, lpv, dv),
            reads=[ln_rg], writes=["dbuf"])
        for (xt, xrg, munm) in ((xr_t, xr_rg, "murs"), (xk_t, xk_rg, "muks")):
            for j in range(NCH):
                add("dve", (lambda j_, xt_, mu_: lambda B: B.dve.scalar_tensor_tensor(
                    out=xt_[:, j_ * TT:j_ * TT + TTt],
                    in0=dbuf[:, j_ * TT:j_ * TT + TTt],
                    scalar=mu_[:, j_:j_ + 1],
                    in1=ln_t[:, j_ * 257:j_ * 257 + TTt],
                    op0=ALU.mult, op1=ALU.add))(j, xt, mu_exp[munm].ap()),
                    reads=["dbuf", "mu_" + munm, ln_rg], writes=[xrg])

    def ffn_proj2(t, xr_t, xr_rg, xk_t, xk_rg):
        TTt = tile_tok(t)

        def proj(wname, x_t, x_rg, dstbuf, dst_rg, func, scale):
            for i in range(NCH):
                bank, slot = proj_bank(i)
                for j in range(NCH):
                    add("pe", (lambda b_, s_, i_, j_, w_, x2, n_: lambda B: B.pe.matmul(
                        PS[b_][:, s_:s_ + n_],
                        W[w_][:, j_ * 1024 + i_ * 128: j_ * 1024 + (i_ + 1) * 128],
                        x2[:, j_ * TT:j_ * TT + n_],
                        start=j_ == 0, stop=j_ == NCH - 1))(bank, slot, i, j, wname, x_t, TTt),
                        reads=[f"w_{wname}_{j}", x_rg], writes=[f"ps_{bank}"])
                add("act", (lambda i_, b_, s_, d_, f_, n_, sc_: lambda B: B.act.activation(
                    out=d_[:, i_ * TT:i_ * TT + n_], in_=PS[b_][:, s_:s_ + n_],
                    func=f_, scale=sc_))(i, bank, slot, dstbuf, func, TTt, scale),
                    reads=[f"ps_{bank}"], writes=[dst_rg])
        proj("wr", xr_t, xr_rg, pool[0], "pb0", AF.Sigmoid, 1.0)
        proj("wk", xk_t, xk_rg, pool[1], "pb1", AF.Relu, 0.7071067811865476)

    def ffn_kact(t):
        TTt = tile_tok(t)
        add("act", (lambda n_: lambda B: B.act.activation(
            out=ch3(pool[2], TT, 0, n_), in_=ch3(pool[1], TT, 0, n_), func=AF.Square))(TTt),
            reads=["pb1"], writes=["pb2"])

    def ffn_v(t, strm, st_rg, add_eng="dve"):
        TTt = tile_tok(t)
        for i in range(NCH):
            bank, slot = proj_bank(i)
            for j in range(NCH):
                add("pe", (lambda b_, s_, i_, j_, n_: lambda B: B.pe.matmul(
                    PS[b_][:, s_:s_ + n_],
                    W["wv"][:, j_ * 1024 + i_ * 128: j_ * 1024 + (i_ + 1) * 128],
                    pool[2][:, j_ * TT:j_ * TT + n_],
                    start=j_ == 0, stop=j_ == NCH - 1))(bank, slot, i, j, TTt),
                    reads=["w_wv_" + str(j), "pb2"], writes=[f"ps_{bank}"])
            add("dve", (lambda i_, b_, s_, n_: lambda B: B.dve.tensor_tensor(
                out=pool[1][:, i_ * TT:i_ * TT + n_],
                in0=pool[0][:, i_ * TT:i_ * TT + n_], in1=PS[b_][:, s_:s_ + n_],
                op=ALU.mult))(i, bank, slot, TTt),
                reads=["pb0", f"ps_{bank}"], writes=["pb1"])
            add(add_eng, (lambda i_, st_, n_: lambda B: B.dve.tensor_tensor(
                out=st_[:, i_ * TT:i_ * TT + n_], in0=pool[1][:, i_ * TT:i_ * TT + n_],
                in1=st_[:, i_ * TT:i_ * TT + n_], op=ALU.add))(i, strm, TTt),
                reads=["pb1", st_rg], writes=[st_rg])

    @_tagged
    def st_P1_stats(t):
        TTt = tile_tok(t)
        strm = stream[t % 3]
        st_rg = f"stream{t % 3}"
        sv = ch3(strm, TT, 0, TTt)
        if t > 0:
            ring_copy(t, ln1, "ln1", 257, 1)
        ln_stats(t, sv, [st_rg], first_writes=[f"xok{t}"], stbank="m1", strow=64)

    @_tagged
    def st_P1_rest(t):
        TTt = tile_tok(t)
        strm = stream[t % 3]
        st_rg = f"stream{t % 3}"
        sv = ch3(strm, TT, 0, TTt)
        ln_rows_bcast(t, stbank="m1", strow=64, rslot=1)
        ln_apply(t, sv, [st_rg], ch3(ln1, 257, 1, TTt), "ln1")
        ln_mixes(t, ln1, "ln1", xr1, "xr1", xk1, "xk1")

    def st_P1(t):
        st_P1_stats(t)
        st_P1_rest(t)

    @_tagged
    def st_M1a(t):
        ffn_proj2(t, xr1, "xr1", xk1, "xk1")
        ffn_kact(t)

    @_tagged
    def st_M1b(t):
        ffn_v(t, stream[t % 3], f"stream{t % 3}")

    @_tagged
    def st_P2_stats(t):
        TTt = tile_tok(t)
        strm = stream[t % 3]
        st_rg = f"stream{t % 3}"
        sv = ch3(strm, TT, 0, TTt)
        if t > 0:
            ring_copy(t, lna, "lna", 272, 16)
            ring_copy(t, krng, "krng", 272, 16)
        if TAPS:
            dtap = taps["tap1"][:, :, toff(t):toff(t) + TTt]
            dma("sync", (lambda d_, s_: lambda B: B.sync.dma_start(out=d_, in_=s_))(dtap, sv),
                reads=[st_rg])
        ln_stats(t, sv, [st_rg], strow=32)

    @_tagged
    def st_P2_rest(t):
        TTt = tile_tok(t)
        strm = stream[t % 3]
        st_rg = f"stream{t % 3}"
        sv = ch3(strm, TT, 0, TTt)
        ln_rows_bcast(t, bcbank="s1", strow=32, rslot=1)
        ln_apply(t, sv, [st_rg], ch3(lna, 272, 16, TTt), "lna", bcbank="s1")

    def st_P2(t):
        st_P2_stats(t)
        st_P2_rest(t)

    @_tagged
    def st_M2(t):
        TTt = tile_tok(t)

        def aproj(wname, dst_t, dst_rg, width, col0):
            for i in range(NCH):
                bank, slot = proj_bank(i)
                for j in range(NCH):
                    add("pe", (lambda b_, s_, i_, j_, w_, n_: lambda B: B.pe.matmul(
                        PS[b_][:, s_:s_ + n_],
                        W[w_][:, j_ * 1024 + i_ * 128: j_ * 1024 + (i_ + 1) * 128],
                        lna[:, j_ * 272 + 16: j_ * 272 + 16 + n_],
                        start=j_ == 0, stop=j_ == NCH - 1))(bank, slot, i, j, wname, TTt),
                        reads=[f"w_{wname}_{j}", "lna"], writes=[f"ps_{bank}"])
                add("act", (lambda i_, b_, s_, d_, wd_, c0, n_: lambda B: B.act.activation(
                    out=d_[:, i_ * wd_ + c0: i_ * wd_ + c0 + n_],
                    in_=PS[b_][:, s_:s_ + n_], func=AF.Copy))(i, bank, slot, dst_t, width, col0, TTt),
                    reads=[f"ps_{bank}"], writes=[dst_rg])
        aproj("wq", qbf, "qbf", TT, 0)
        aproj("wkk", krng, "krng", 272, 16)

    @_tagged
    def st_A2(t):
        TTt = tile_tok(t)

        def emit_subtile(s, o):
            vt = vtm[s % 2]
            vt_rg = f"vtm{s % 2}"
            for h in range(2):
                bank = "a" if h == 0 else "b"
                for j in range(NCH):
                    add("pe", (lambda b_, h_, j_, o_: lambda B: B.pe.matmul(
                        PS[b_][:, 0:512],
                        lna[:, j_ * 272 + 16 + o_: j_ * 272 + 16 + o_ + 128],
                        W["wkv"][:, j_ * 1024 + h_ * 512: j_ * 1024 + (h_ + 1) * 512],
                        start=j_ == 0, stop=j_ == NCH - 1))(bank, h, j, o),
                        reads=["lna", f"w_wkv_{j}"], writes=[f"ps_{bank}"])
                add("act", (lambda b_, h_, v_: lambda B: B.act.activation(
                    out=v_[:, h_ * 512:(h_ + 1) * 512], in_=PS[b_][:, 0:512], func=AF.Copy))(bank, h, vt),
                    reads=[f"ps_{bank}"], writes=[vt_rg])
            def ph_tau(g):
                sbank = "s1" if g % 2 == 0 else "s2"
                for hh in range(4):
                    h = g * 4 + hh
                    i = h // 2
                    pb = (h % 2) * 64
                    add("pe", (lambda sb_, hh_, i_, pb_, o_: lambda B: B.pe.matmul(
                        PS[sb_][:, hh_ * 128:(hh_ + 1) * 128],
                        krng[pb_:pb_ + 64, i_ * 272 + 16 + o_: i_ * 272 + 16 + o_ + 128],
                        qbf[pb_:pb_ + 64, i_ * TT + o_: i_ * TT + o_ + 128],
                        start=True, stop=True))(sbank, hh, i, pb, o),
                        reads=["krng", "qbf"], writes=[f"ps_{sbank}"])
                    add("pe", (lambda h_, i_, pb_, o_: lambda B: B.pe.matmul(
                        PS["bc"][0:16, h_ * 16:(h_ + 1) * 16],
                        krng[pb_:pb_ + 64, i_ * 272 + o_: i_ * 272 + o_ + 16],
                        qbf[pb_:pb_ + 64, i_ * TT + o_: i_ * TT + o_ + 16],
                        start=True, stop=True))(h, i, pb, o),
                        reads=["krng", "qbf"], writes=["ps_bc"])

            def ph_sig(g):
                sbank = "s1" if g % 2 == 0 else "s2"
                tm = tauM[g % 2]
                add("act", (lambda sb_, tm_: lambda B: B.act.activation(
                    out=tm_[:], in_=PS[sb_][:], func=AF.Sigmoid, scale=SCALE))(sbank, tm),
                    reads=[f"ps_{sbank}"], writes=[f"tauM{g % 2}"])

            def ph_mask(g):
                tm = tauM[g % 2]
                tmv = tm.ap().rearrange("p (h w) -> p h w", h=4)
                mb = mcur.ap().unsqueeze(1).broadcast_to([P, 4, 128])
                add("pool", (lambda tmv_, mb_: lambda B: B.pool.tensor_tensor(
                    out=tmv_, in0=tmv_, in1=mb_, op=ALU.mult))(tmv, mb),
                    reads=[f"tauM{g % 2}", "mcur"], writes=[f"tauM{g % 2}"])

            def ph_msg(g):
                tm = tauM[g % 2]
                tm_rg = f"tauM{g % 2}"
                for hh in range(4):
                    h = g * 4 + hh
                    mbank = ("m1" if h < 8 else "a") if h % 2 == 0 else ("m2" if h < 8 else "b")
                    cs = ((h // 2) % 4) * 128
                    first = h in (0, 1, 8, 9)
                    add("pe", (lambda mb2, cs_, h_, tm_, f_, vt_: lambda B: B.pe.matmul(
                        PS[mb2][0:64, cs_:cs_ + 128],
                        vt_[:, h_ * 64:(h_ + 1) * 64],
                        tm_[:, (h_ % 4) * 128:((h_ % 4) + 1) * 128],
                        start=f_, stop=False))(mbank, cs, h, tm, first, vt),
                        reads=[vt_rg, tm_rg], writes=[f"ps_{mbank}"])

            ph_tau(0); ph_tau(1); ph_sig(0); ph_mask(0)
            ph_tau(2); ph_sig(1); ph_mask(1); ph_msg(0)
            ph_tau(3); ph_sig(2); ph_mask(2); ph_msg(1)
            ph_sig(3); ph_mask(3); ph_msg(2); ph_msg(3)
            add("act", lambda B: B.act.activation(
                out=tauP[:], in_=PS["bc"][0:16, 0:256], func=AF.Sigmoid, scale=SCALE),
                reads=["ps_bc"], writes=["tauP"])
            add("pool", lambda B: B.pool.tensor_tensor(
                out=tauP[:], in0=tauP[:], in1=mprev[:], op=ALU.mult),
                reads=["tauP", "mprev"], writes=["tauP"])
            for h in range(HEADS):
                mbank = ("m1" if h < 8 else "a") if h % 2 == 0 else ("m2" if h < 8 else "b")
                cs = ((h // 2) % 4) * 128
                add("pe", (lambda mb2, cs_, h_, e_: lambda B: B.pe.matmul(
                    PS[mb2][0:64, cs_:cs_ + 16],
                    vbnd[:, h_ * 64:(h_ + 1) * 64],
                    tauP[:, h_ * 16:(h_ + 1) * 16],
                    start=False, stop=e_))(mbank, cs, h, h in (6, 7, 14, 15)),
                    reads=["vbnd", "tauP"], writes=[f"ps_{mbank}"])
            for mbank, pb0, cset in (("m1", 0, 0), ("m2", 64, 0), ("a", 0, 4), ("b", 64, 4)):
                dstm = qbf.ap().rearrange("p (c w) -> p c w", c=NCH)[
                    pb0:pb0 + 64, cset:cset + 4, o:o + 128]
                srcm = PS[mbank].ap()[0:64, :].rearrange("p (c w) -> p c w", c=4)
                add("act", (lambda s_, d_: lambda B: B.act.activation(
                    out=d_, in_=s_, func=AF.Copy))(srcm, dstm),
                    reads=[f"ps_{mbank}"], writes=["qbf"])
            dma("pool", (lambda v_: lambda B: B.pool.dma_start(out=vbnd[:], in_=v_[112:128, :]))(vt),
                reads=[vt_rg], writes=["vbnd"])

        for s in range(TTt // 128):
            emit_subtile(s, s * 128)

    @_tagged
    def st_M2b(t):
        TTt = tile_tok(t)
        for i in range(NCH):
            bank, slot = proj_bank(i)
            for j in range(NCH):
                add("pe", (lambda b_, s_, i_, j_, n_: lambda B: B.pe.matmul(
                    PS[b_][:, s_:s_ + n_],
                    W["wo"][:, j_ * 1024 + i_ * 128: j_ * 1024 + (i_ + 1) * 128],
                    qbf[:, j_ * TT:j_ * TT + n_],
                    start=j_ == 0, stop=j_ == NCH - 1))(bank, slot, i, j, TTt),
                    reads=[f"w_wo_{j}", "qbf"], writes=[f"ps_{bank}"])
            add("dve", (lambda i_, b_, s_, n_: lambda B: B.dve.tensor_tensor(
                out=xr1[:, i_ * TT:i_ * TT + n_],
                in0=lna[:, i_ * 272 + 16: i_ * 272 + 16 + n_],
                in1=PS[b_][:, s_:s_ + n_], op=ALU.add))(i, bank, slot, TTt),
                reads=["lna", f"ps_{bank}"], writes=["xr1"])

    @_tagged
    def st_P2e_stats(t):
        TTt = tile_tok(t)
        yv = ch3(xr1, TT, 0, TTt)
        ln_stats(t, yv, ["xr1"], src_flat=xr1)

    @_tagged
    def st_P2e_rows(t):
        ln_rows_bcast(t)

    @_tagged
    def st_P2e_apply(t):
        TTt = tile_tok(t)
        strm = stream[t % 3]
        st_rg = f"stream{t % 3}"
        sv = ch3(strm, TT, 0, TTt)
        yv = ch3(xr1, TT, 0, TTt)
        ln_apply(t, yv, ["xr1"], ch3(dbuf, TT, 0, TTt), "dbuf")
        for j in range(NCH):
            add("dve", (lambda j_, n_: lambda B: B.dve.tensor_scalar(
                out=xk1[:, j_ * TT:j_ * TT + n_], in0=dbuf[:, j_ * TT:j_ * TT + n_],
                scalar1=cols_sb["gammas"][:, j_:j_ + 1], scalar2=cols_sb["betas"][:, j_:j_ + 1],
                op0=ALU.mult, op1=ALU.add))(j, TTt),
                reads=["dbuf", "gammas", "betas"], writes=["xk1"])
        add("dve", (lambda sv_, o3: lambda B: B.dve.tensor_tensor(
            out=sv_, in0=sv_, in1=o3, op=ALU.add))(sv, ch3(xk1, TT, 0, TTt)),
            reads=[st_rg, "xk1"], writes=[st_rg])

    @_tagged
    def st_P3_stats(t):
        TTt = tile_tok(t)
        strm = stream[t % 3]
        st_rg = f"stream{t % 3}"
        sv = ch3(strm, TT, 0, TTt)
        if TAPS:
            dtap2 = taps["tap2"][:, :, toff(t):toff(t) + TTt]
            dma("sync", (lambda d_, s_: lambda B: B.sync.dma_start(out=d_, in_=s_))(dtap2, sv),
                reads=[st_rg])
        if t > 0:
            ring_copy(t, ln3, "ln3", 257, 1, masked=(t == 1))
        ln_stats(t, sv, [st_rg], strow=64)

    @_tagged
    def st_P3_rest(t):
        TTt = tile_tok(t)
        strm = stream[t % 3]
        st_rg = f"stream{t % 3}"
        sv = ch3(strm, TT, 0, TTt)
        ln_rows_bcast(t, bcbank="s1", strow=64, rslot=0)
        ln_apply(t, sv, [st_rg], ch3(ln3, 257, 1, TTt), "ln3", bcbank="s1")
        ln_mixes(t, ln3, "ln3", xr3, "xr3", xk3, "xk3")

    @_tagged
    def st_M3a(t):
        ffn_proj2(t, xr3, "xr3", xk3, "xk3")
        ffn_kact(t)

    @_tagged
    def st_M3b(t):
        ffn_v(t, stream[t % 3], f"stream{t % 3}", add_eng="pool")

    @_tagged
    def st_OUT(t):
        TTt = tile_tok(t)
        sv = ch3(stream[t % 3], TT, 0, TTt)
        dsto = out_v[:, :, (t - 1) * TT:(t - 1) * TT + TTt]
        dma("sync", (lambda d_, s_: lambda B: B.sync.dma_start(out=d_, in_=s_))(dsto, sv),
            reads=[f"stream{t % 3}"])

    def wload(names):
        for nm in names:
            for j in range(NCH):
                src_ = wts_in[nm][j * 128:(j + 1) * 128, :]
                dst_ = W[nm][:, j * 1024:(j + 1) * 1024]
                dma("pool", (lambda d_, s_: lambda B: B.pool.dma_start(out=d_, in_=s_))(dst_, src_),
                    writes=[f"w_{nm}_{j}"])

    wload(("wr", "wk"))
    st_xdma(0)
    st_P1(0)
    st_M1a(0)
    wload(("wv",))
    st_M1b(0)
    wload(("wq", "wkk", "wkv"))
    dma("pool", lambda B: B.pool.dma_start(out=mcur[:], in_=mcur_in), writes=["mcur"])
    dma("pool", lambda B: B.pool.dma_start(out=mprev[:], in_=mprev_in), writes=["mprev"])
    st_P2(0)
    st_M2(0)
    wload(("wo",))
    st_xdma(1)
    st_P1(1)
    st_A2(0)
    for t in range(N_TILES):
        nxt = t + 1
        if nxt < N_TILES:
            st_M1a(nxt)
        if nxt + 1 < N_TILES:
            st_xdma(nxt + 1)
        st_M2b(t)
        st_P2e_stats(t)
        if nxt < N_TILES:
            st_M1b(nxt)
        st_P2e_rows(t)
        st_P2e_apply(t)
        if nxt < N_TILES:
            st_P2_stats(nxt)
        st_P3_stats(t)
        if nxt < N_TILES:
            st_P2_rest(nxt)
        if nxt + 1 < N_TILES:
            st_P1_stats(nxt + 1)
        if nxt < N_TILES:
            st_M2(nxt)
        st_P3_rest(t)
        if nxt < N_TILES:
            st_A2(nxt)
        if nxt + 1 < N_TILES:
            st_P1_rest(nxt + 1)
        st_M3a(t)
        st_M3b(t)
        if t > 0:
            st_OUT(t)

    per_eng = prog.emit_plan()

    class BB:
        def __init__(self, eng):
            self.pool = self.sync = self.pe = self.act = self.dve = eng

    with nc.Block() as block:
        for bname in ("sync", "gpsimd", "tensor", "scalar", "vector"):
            ops = per_eng[bname]

            def body(eng, ops=ops):
                B = BB(eng)
                for op in ops:
                    for semname, val in op["waits"]:
                        eng.wait_ge(sems[semname], val)
                    inst = op["fn"](B)
                    if op["inc"] is not None:
                        inst.then_inc(sems[op["inc"][0]], op["inc"][1])
            getattr(block, bname)(body)

        def closer(eng):
            for sn, tot in prog.dma_totals.items():
                if tot:
                    eng.wait_ge(sems[sn], tot)
        block.sync(closer)

    for g in reversed(ctx):
        g.__exit__(None, None, None)
    return nc


_NC_CACHE = {}


def get_program():
    if "nc" not in _NC_CACHE:
        _NC_CACHE["nc"] = build_nc()
    return _NC_CACHE["nc"]


def kernel(L0_attn_x, L0_attn_wq, L0_attn_wkv, L0_attn_wo, L0_attn_gamma, L0_attn_beta,
           L0_ffn_W_r, L0_ffn_W_k, L0_ffn_W_v, L0_ffn_mu_r, L0_ffn_mu_k):
    Bb, T, D_ = L0_attn_x.shape
    assert (Bb, T, D_) == (2, 8192, 1024)
    NCORES = 8
    x_flat = np.ascontiguousarray(np.asarray(L0_attn_x, dtype=np.float32)).reshape(Bb * T, D_)
    x_fm = np.ascontiguousarray(x_flat.T)

    wqT = np.ascontiguousarray(np.asarray(L0_attn_wq, dtype=np.float32).T)
    wkv = np.asarray(L0_attn_wkv, dtype=np.float32)
    kidx = np.array([h * 128 + j for h in range(HEADS) for j in range(HD)])
    vidx = np.array([h * 128 + HD + j for h in range(HEADS) for j in range(HD)])
    wkkT = np.ascontiguousarray(wkv[kidx, :].T)
    wkvT = np.ascontiguousarray(wkv[vidx, :].T)
    woT = np.ascontiguousarray(np.asarray(L0_attn_wo, dtype=np.float32).T)
    wrT = np.ascontiguousarray(np.asarray(L0_ffn_W_r, dtype=np.float32).T)
    wkT = np.ascontiguousarray(np.asarray(L0_ffn_W_k, dtype=np.float32).T)
    wvT = np.ascontiguousarray(np.asarray(L0_ffn_W_v, dtype=np.float32).T)

    def colsfmt(v):
        return np.ascontiguousarray(np.asarray(v, dtype=np.float32).reshape(NCH, P).T)

    murs, muks = colsfmt(L0_ffn_mu_r), colsfmt(L0_ffn_mu_k)
    gams, bets = colsfmt(L0_attn_gamma), colsfmt(L0_attn_beta)

    jj = np.arange(P)[None, :]
    uu = np.arange(P)[:, None]
    mask_cur = (((jj - uu) >= 1) & ((jj - uu) <= WIN)).astype(np.float32)
    t16 = np.arange(16)[None, :]
    u16 = np.arange(16)[:, None]
    mp = (u16 >= t16).astype(np.float32)
    mask_prev = np.ascontiguousarray(np.tile(mp, (1, HEADS)))

    in_maps = []
    for c in range(NCORES):
        seq, q = c // 4, c % 4
        s0 = q * 2048
        base = seq * 8192
        if s0 - T_HALO < 0:
            xs = np.concatenate([np.zeros((D_, T_HALO), np.float32),
                                 x_fm[:, base: base + T_OWN]], axis=1)
        else:
            xs = x_fm[:, base + s0 - T_HALO: base + s0 + T_OWN]
        in_maps.append({
            "x_sl": np.ascontiguousarray(xs),
            "wrT": wrT, "wkT": wkT, "wvT": wvT, "wqT": wqT,
            "wkkT": wkkT, "wkvT": wkvT, "woT": woT,
            "murs": murs, "muks": muks, "gammas": gams, "betas": bets,
            "mask_cur": mask_cur, "mask_prev": mask_prev,
            "bnd_mask": np.full((P, 1), 0.0 if q == 0 else 1.0, np.float32),
        })

    nc = get_program()
    res = run_bass_kernel_spmd(nc, in_maps, core_ids=list(range(NCORES)))
    out_fm = np.concatenate([np.asarray(r["out_fm"]) for r in res.results], axis=1)
    out = np.ascontiguousarray(out_fm.T).reshape(Bb, T, D_).astype(np.float32)
    return out
